# version 10
# baseline (speedup 1.0000x reference)
import contextlib
import numpy as np
import concourse.bass as bass
import concourse.mybir as mybir
from concourse.alu_op_type import AluOpType as Op

F32 = mybir.dt.float32
F32R = mybir.dt.float32r
BF16 = mybir.dt.bfloat16
AF = mybir.ActivationFunctionType
AX = mybir.AxisListType.X
MAGIC = float(3 * 2**22)
DECAY = -0.6065306597126334
EPS_Q = 1e-5
LN_EPS = 1e-5
GN_EPS = 64 * 1e-5

D = 1024
I_FF = 4096
H = 16
HD = 64
C = 64
NDC = D // 128
NIC = I_FF // 128


class Sem:
    def __init__(self, nc, es, name):
        self.h = es.enter_context(nc.semaphore(name))
        self.n = 0

    def inc(self, inst, k=16):
        inst.then_inc(self.h, k)
        self.n += k
        return inst


READ_KEYS = ("in_", "in0", "in1", "lhsT", "rhs", "identity", "scalar", "scalar1",
             "scalar2", "bias", "scale", "data0", "data1", "initial", "mask",
             "on_true", "on_false", "weights", "in_max", "in_values")
WRITE_KEYS = ("out", "accum_out", "ap", "dest", "out_max", "out_indices")


def _ap_region(ap):
    t = ap.tensor
    shape = t.shape
    if type(t).__name__ == "PSumTensorHandle":
        free = 1
        for s_ in shape[1:]:
            free *= s_
        p_lo, p_hi = 0, shape[0]
        dims = list(ap.ap)
        off = ap.offset
        if isinstance(off, int) and dims and free > 0:
            step0, cnt0 = dims[0]
            if step0 == free:
                p_lo = off // free
                p_hi = p_lo + cnt0
        return ((t.name, "ps"), p_lo, p_hi, 0, free)
    free = 1
    for s in shape[1:]:
        free *= s
    dims = list(ap.ap)
    off = ap.offset
    if not isinstance(off, int):
        off = 0
        return (t.name, 0, shape[0], 0, free)
    p_lo, p_hi, f_lo, f_hi = 0, shape[0], 0, free
    if dims and free > 0:
        step0, cnt0 = dims[0]
        if step0 == free or (step0 == 0 and cnt0 == 1) or len(shape) == 1:
            p_lo = off // free if free else 0
            p_hi = p_lo + cnt0
            rest = dims[1:]
            f_lo = off % free if free else 0
            span = 1
            for st, cn in rest:
                span += (cn - 1) * abs(st)
            f_hi = f_lo + span
    return (t.name, p_lo, p_hi, f_lo, f_hi)


def _overlap(a, b):
    if a[0] != b[0]:
        return False
    if isinstance(a[0], tuple):
        return True
    return a[1] < b[2] and b[1] < a[2] and a[3] < b[4] and b[3] < a[4]


class Dep:
    __slots__ = ("sem", "n", "region", "dma", "src")
    def __init__(self, sem, n, region, dma=None, src=None):
        self.sem = sem
        self.n = n
        self.region = region
        self.dma = dma
        self.src = src

    def wait_val(self):
        return self.dma.n if self.dma is not None else self.n

    def cap(self):
        return self.dma.n if self.dma is not None else (self.src.n_all if self.src else None)


class Eng:
    def __init__(self, ctx, name):
        self.ctx = ctx
        self.name = name
        self.e = getattr(ctx.nc, name)
        self.self_sem = ctx.es.enter_context(ctx.nc.semaphore(f"sem_{name}"))
        self.n_all = 0
        self.n_marked = 0
        self.n_ops = 0
        self.waited = {}
        self.S = self
        tg = ctx.wait_targets
        if tg is not None:
            self.marked_sorted = sorted(tg.get(name, set()))
            self.marked_set = set(self.marked_sorted)
        else:
            self.marked_sorted = []
            self.marked_set = set()
        ctx.eng_by_sem[id(self.self_sem)] = self

    @property
    def h(self):
        return self.self_sem

    @property
    def n(self):
        return self.n_all

    def inc(self, inst, k=1):
        return inst

    def _wait(self, sem, val, cap=None):
        key = id(sem)
        tgt = self.ctx.wait_targets
        if key in self.ctx.eng_by_sem:
            if tgt is None:
                self.ctx.rec_targets.setdefault(
                    self.ctx.eng_by_sem[key].name, set()).add(val)
            else:
                peng = self.ctx.eng_by_sem[key]
                marked = peng.marked_sorted
                import bisect
                i = bisect.bisect_left(marked, val)
                assert i < len(marked) and marked[i] == val, (
                    self.name, peng.name, val, marked[max(0, i - 2):i + 2])
                val = i + 1
        if self.waited.get(key, -1) >= val:
            return
        self.e.wait_ge(sem, val)
        self.waited[key] = val
        self.n_ops += 1
        self.n_waits = getattr(self, "n_waits", 0) + 1

    def wait(self, sem, val=None):
        self._wait(sem.h, sem.n if val is None else val)

    def _sync_deps(self, reads, writes):
        tr = self.ctx.track
        need = {}
        skip_self = self.name == "tensor" and not getattr(self.ctx, "no_tskip", False)
        def _need(d, my_region=None):
            if skip_self and d.src is self:
                if (my_region is None or
                        (my_region[1] == d.region[1] and my_region[2] == d.region[2])):
                    return
            key = id(d.sem)
            v, cp_ = d.wait_val(), d.cap()
            if key not in need or need[key][0] < v:
                need[key] = (v, cp_, d.sem)
        for r in reads:
            for d in tr.get(r[0], {}).get("w", []):
                if _overlap(r, d.region):
                    _need(d, r)
        for w in writes:
            ent = tr.get(w[0], {})
            for d in ent.get("w", []):
                if _overlap(w, d.region):
                    _need(d, w)
            for d in ent.get("r", []):
                if _overlap(w, d.region):
                    _need(d, w)
        for v, cp_, semh in need.values():
            self._wait(semh, v, cp_)

    @staticmethod
    def _prune(lst, key):
        best = {}
        for d in lst:
            k = id(d.sem)
            if k not in best or best[k].n < d.n:
                best[k] = d
        whole = (key, 0, 1 << 30, 0, 1 << 30)
        return [Dep(d.sem, d.n, whole, d.dma, d.src) for d in best.values()]

    def _record(self, reads, writes, sem, n, dma=None, src_eng=None):
        tr = self.ctx.track
        for r in reads:
            ent = tr.setdefault(r[0], {"w": [], "r": []})
            ent["r"].append(Dep(sem, n, r, dma, src_eng))
            if len(ent["r"]) > 24:
                ent["r"] = self._prune(ent["r"], r[0])
        for w in writes:
            ent = tr.setdefault(w[0], {"w": [], "r": []})
            ent["w"].append(Dep(sem, n, w, dma, src_eng))
            if len(ent["w"]) > 24:
                ent["w"] = self._prune(ent["w"], w[0])

    def op(self, fname, *args, **kwargs):
        assert not args, "use kwargs only"
        reads, writes = [], []
        rowgrp = None
        if fname in ("matmul", "transpose"):
            lap = kwargs.get("lhsT", kwargs.get("in_"))
            lr = _ap_region(lap)
            rowgrp = (lr[1], lr[2])
        for k, vv in kwargs.items():
            if isinstance(vv, bass.AP):
                if k in WRITE_KEYS:
                    reg = _ap_region(vv)
                    if rowgrp is not None and isinstance(reg[0], tuple):
                        reg = (reg[0], rowgrp[0], rowgrp[1], reg[3], reg[4])
                    writes.append(reg)
                elif k in READ_KEYS:
                    reads.append(_ap_region(vv))
        self._sync_deps(reads, writes)
        inst = getattr(self.e, fname)(**kwargs)
        self.n_all += 1
        if self.ctx.wait_targets is None or self.n_all in self.marked_set:
            inst.then_inc(self.self_sem, 1)
            self.n_marked += 1
        self.n_ops += 1
        self._record(reads, writes, self.self_sem, self.n_all, src_eng=self)
        return inst

    def dma(self, out, in_, sem=None, k=16, **kw):
        pool = self.ctx.dma_pool
        sem = pool[self.ctx.dma_rr % len(pool)]
        self.ctx.dma_rr += 1
        reads = [_ap_region(in_)]
        writes = [_ap_region(out)]
        self._sync_deps(reads, writes)
        inst = self.e.dma_start(out=out, in_=in_, **kw)
        sem.inc(inst, k)
        self.n_ops += 1
        self._record(reads, writes, sem.h, sem.n, dma=sem)
        return inst


class Ctx:
    def __init__(self, nc, dev=True):
        self.nc = nc
        self.dev = dev
        self.es = contextlib.ExitStack()
        self.ps_es = None
        self.phase_es = None
        self.bufs = {}
        self.dma_sems = []
        self.track = {}
        self.wait_hist = {}
        self._uid = 0
        self.dma_pool = [Sem(nc, self.es, f"dmap{i}") for i in range(16)]
        self.dma_rr = 0
        self.wait_targets = None
        self.rec_targets = {}
        self.eng_by_sem = {}

    def uid(self):
        self._uid += 1
        return self._uid

    def sem(self, name):
        s = Sem(self.nc, self.es, f"{name}_{self.uid()}")
        self.dma_sems.append(s)
        return s

    def sbuf(self, name, shape, dtype, phase=False):
        es = self.phase_es if phase else self.es
        t = es.enter_context(self.nc.sbuf_tensor(f"{name}", list(shape), dtype))
        self.bufs[name] = t
        return t

    def psum(self, name, shape, dtype=F32, phase=True):
        es = self.phase_es if phase else self.es
        t = es.enter_context(self.nc.psum_tensor(f"{name}", list(shape), dtype))
        self.bufs[name] = t
        return t

    @contextlib.contextmanager
    def phase(self, name):
        self.phase_es = contextlib.ExitStack()
        try:
            yield
        finally:
            self.fence()
            self.phase_es.close()
            self.phase_es = None

    def fence(self):
        engs = [self.t, self.v, self.s, self.g, self.y]
        for e in engs:
            for o in engs:
                if o is not e and (o.S.n if True else 0) > 0:
                    e.wait(o.S)
            for sem in self.dma_sems + self.dma_pool:
                if sem.n > 0:
                    e.wait(sem)

    def warm(self):
        ident = self.bufs.get("ident_b")
        if ident is not None:
            self.t.e.ldweights(ident[:, 0:64])
            self.t.n_ops += 1

    def engines(self):
        self.t = Eng(self, "tensor")
        self.v = Eng(self, "vector")
        self.s = Eng(self, "scalar")
        self.g = Eng(self, "gpsimd")
        self.y = Eng(self, "sync")
        return self


def apx(buf, p0, np_, col0, dims):
    if isinstance(buf, bass.AP):
        h = buf.tensor
        base = buf.offset
    else:
        h = buf
        base = 0
    fs = 1
    for s_ in h.shape[1:]:
        fs *= s_
    return bass.AP(h, base + p0 * fs + col0, [[fs, np_]] + [list(d) for d in dims])



def ph_consts(cx, io):
    c = cx.bufs
    names_bf = [("ident_b", [128, 128]), ("ones_row", [1, 128]),
                ("kkb", [128, D]), ("kab", [128, D]), ("okab", [128, D]),
                ("rkb", [128, D]), ("gnwb", [128, D]), ("gnbb", [128, D]),
                ("wB_aug", [65, D]), ("aB_aug", [65, D]), ("gB_c", [128, 2 * D]),
                ("wA_c", [128, NDC * 64]), ("aA_c", [128, NDC * 64]),
                ("gA_c", [128, NDC * 160]),
                ("maskL", [64, 64]), ("maskS", [64, 64])]
    ld = cx.sem("c_ld")
    for nm, shp in names_bf:
        cx.sbuf(nm, shp, BF16)
        cx.y.dma(c[nm][:], io[nm], ld)
    cx.sbuf("mix6", [128, 6 * NDC], F32)
    cx.y.dma(c["mix6"][:], io["mix6"], ld)
    cx.sbuf("wAd_c", [128, NDC * 64], BF16)
    cx.sbuf("aAd_c", [128, NDC * 64], BF16)
    cx.sbuf("gAd_c", [128, NDC * 160], BF16)
    cx.sbuf("gsb", [128, 1], F32)
    cx.sbuf("swdb", [128, 1], F32)
    magic = cx.sbuf("magicb", [128, 1], F32)
    cx.v.op("memset", ap=magic[:], constant=MAGIC)
    nmagic = cx.sbuf("nmagicb", [128, 1], F32)
    cx.v.op("memset", ap=nmagic[:], constant=-MAGIC)
    epsb = cx.sbuf("epsb", [128, 1], F32)
    cx.v.op("memset", ap=epsb[:], constant=LN_EPS)
    gnepsb = cx.sbuf("gnepsb", [128, 1], F32)
    cx.v.op("memset", ap=gnepsb[:], constant=GN_EPS)
    cx.c_ld = ld



def ph_wq(cx, io, mid=None):
    c = cx.bufs
    t, v, s, g, y = cx.t, cx.v, cx.s, cx.g, cx.y
    specs = [
        (io["wg"], I_FF, D, None, 0),
        (io["wu"], I_FF, D, None, 1),
        (io["wd"], D, I_FF, None, 2),
    ]
    wtileB = [cx.sbuf(f"wq_tile{i}", [128, I_FF], F32, phase=True) for i in range(2)]
    wqbB = [cx.sbuf(f"wq_qb{i}", [128, I_FF], BF16, phase=True) for i in range(2)]
    stg2B = [cx.sbuf(f"wq_stg2{i}", [128, 2 * I_FF], BF16, phase=True) for i in range(2)]
    acc = cx.sbuf("wq_acc", [128, 4], F32, phase=True)
    red = cx.sbuf("wq_red", [128, 1], F32, phase=True)
    sc = cx.sbuf("wq_sc", [128, 4], F32, phase=True)
    scal = cx.sbuf("wq_scal", [1, 8], F32, phase=True)
    onesb = cx.sbuf("ones_col", [128, 1], F32, phase=True)
    onesr_f = cx.sbuf("ones_row_f", [1, 128], F32, phase=True)
    ps_t = cx.psum("wq_ps", [128, 1024], BF16)
    ps_t2 = cx.psum("wq_ps2", [128, 1024], BF16)
    ps_s = cx.psum("wq_ss", [128, 16], F32)

    ld = cx.sem("wq_ld")
    vd = cx.v.S
    sd = cx.s.S
    td = cx.t.S
    st = cx.sem("wq_st")

    v.wait(cx.c_ld)
    v.op("memset", ap=onesb[:], constant=1.0)
    v.op("memset", ap=onesr_f[:], constant=1.0)
    v.op("memset", ap=acc[:], constant=0.0)

    pi = 0
    for (src, R, Cc, dst, mi) in specs:
        for r in range(R // 128):
            wtile = wtileB[pi % 2]
            pi += 1
            y.dma(wtile[:, 0:Cc], src[r * 128:(r + 1) * 128, :], ld)
            v.op("tensor_reduce", out=red[:], in_=wtile[:, 0:Cc], axis=AX, op=Op.add,
                 apply_absolute_value=True)
            vd.inc(v.op("tensor_tensor", out=acc[:, mi:mi + 1], in0=acc[:, mi:mi + 1],
                        in1=red[:], op=Op.add))
    if mid is not None:
        mid()
    t.wait(vd)
    td.inc(t.op("matmul", out=ps_s[0:1, 0:4], lhsT=onesb[:], rhs=acc[:, 0:4],
                start=True, stop=True))
    v.wait(td)
    vd.inc(v.op("tensor_scalar", out=scal[0:1, 0:4], in0=ps_s[0:1, 0:4],
                scalar1=1.0 / 4194304.0, scalar2=EPS_Q, op0=Op.mult, op1=Op.max))
    vd.inc(v.op("reciprocal", out=scal[0:1, 4:8], in_=scal[0:1, 0:4]))
    vd.inc(v.op("tensor_tensor", out=scal[0:1, 3:4], in0=scal[0:1, 0:1],
                in1=scal[0:1, 0:1], op=Op.mult))
    vd.inc(v.op("tensor_tensor", out=scal[0:1, 3:4], in0=scal[0:1, 3:4],
                in1=scal[0:1, 1:2], op=Op.mult))
    t.wait(vd)
    td.inc(t.op("matmul", out=ps_s[:, 8:16], lhsT=onesr_f[0:1, 0:128],
                rhs=scal[0:1, 0:8], start=True, stop=True))
    v.wait(td)
    vd.inc(v.op("tensor_copy", out=sc[:, 0:4], in_=ps_s[:, 12:16]))
    vd.inc(v.op("tensor_copy", out=c["gsb"][:], in_=ps_s[:, 11:12]))
    vd.inc(v.op("tensor_copy", out=c["swdb"][:], in_=ps_s[:, 10:11]))

    if "dbg" in io:
        g.wait(vd, vd.n)
        y.dma(io["dbg"][0:128, 0:4], sc[:, 0:4], st)
        y.dma(io["dbg"][0:1, 4:12], scal[0:1, 0:8], st)
        y.dma(io["dbg"][0:128, 12:16], acc[:, 0:4], st)
    for (src, R, Cc, dst, mi) in specs:
        nct = Cc // 128
        for r in range(R // 128):
            wtile, wqb = wtileB[pi % 2], wqbB[pi % 2]
            stg2 = stg2B[(r // 2) % 2]
            pst = (ps_t, ps_t2)[pi % 2]
            pi += 1
            y.dma(wtile[:, 0:Cc], src[r * 128:(r + 1) * 128, :], ld)
            s.op("activation", out=wtile[:, 0:Cc], in_=wtile[:, 0:Cc],
                 func=AF.Identity, bias=c["magicb"][:], scale=sc[:, mi:mi + 1])
            s.op("activation", out=wtile[:, 0:Cc], in_=wtile[:, 0:Cc],
                 func=AF.Identity, bias=c["nmagicb"][:])
            v.op("tensor_scalar", out=wqb[:, 0:Cc], in0=wtile[:, 0:Cc],
                 scalar1=1.0, scalar2=-1.0, op0=Op.min, op1=Op.max)
            for cb0 in range(0, nct, 8):
                nb = min(8, nct - cb0)
                for j in range(nb):
                    t.op("transpose", out=pst[:, j * 128:(j + 1) * 128],
                         in_=wqb[:, (cb0 + j) * 128:(cb0 + j + 1) * 128],
                         identity=c["ident_b"][:])
                v.op("tensor_copy",
                     out=apx(stg2, 0, 128, cb0 * 256 + (r % 2) * 128, [(256, nb), (1, 128)]),
                     in_=pst.ap().rearrange("p (b q) -> p b q", q=128)[:, 0:nb])
            if r % 2 == 1:
                if mi < 2:
                    d5 = io["wguT"][:, r // 2, mi * 8:mi * 8 + 8, :]
                else:
                    d5 = io["wdT5"][:, r // 2, 0:NIC, :]
                y.dma(d5, stg2.ap().rearrange("p (cb q) -> p cb q", q=256)[:, 0:nct], st)
    cx.wq_st = st



def ph_ffn(cx, io, x_in, x_out, T, tag):
    c = cx.bufs
    t, v, s, g, y = cx.t, cx.v, cx.s, cx.g, cx.y
    GT = 4
    NT = T // 128
    NG = NT // GT
    NSL = I_FF // 512
    NSLB = I_FF // 256

    xt = cx.sbuf(f"{tag}_x", [128, D], F32, phase=True)
    xtD = cx.sbuf(f"{tag}_xD", [128, D], F32, phase=True)
    nxTB = [cx.sbuf(f"{tag}_nxT{i}", [128, GT * D], BF16, phase=True) for i in range(2)]
    hbuf = cx.sbuf(f"{tag}_h", [128, GT * I_FF], F32, phase=True)
    nhT = cx.sbuf(f"{tag}_nhT", [128, GT * I_FF // 32 * 32], BF16, phase=True)
    wgsB = [cx.sbuf(f"{tag}_wgs{i}", [128, 2 * NDC * 256], BF16, phase=True) for i in range(2)]
    nhq = cx.sbuf(f"{tag}_nhq", [128, 1024], BF16, phase=True)
    wdsB = [cx.sbuf(f"{tag}_wds{i}", [128, NIC * 256], BF16, phase=True) for i in range(2)]
    nx = cx.sbuf(f"{tag}_nx", [128, D], F32, phase=True)
    nxq = cx.sbuf(f"{tag}_nxq", [128, D], BF16, phase=True)
    hch = cx.sbuf(f"{tag}_hch", [128, 256], F32, phase=True)
    sm = cx.sbuf(f"{tag}_sm", [128, 16 * GT * 2], F32, phase=True)
    ps_gu = [cx.psum(f"{tag}_gu{i}", [128, 512], F32) for i in range(6)]
    ps_tr = cx.psum(f"{tag}_tr", [128, 1024], BF16)
    ps_trA = cx.psum(f"{tag}_trA", [128, 1024], BF16)
    ps_dn = ps_gu

    ld = cx.sem(f"{tag}_ld")
    wl = cx.sem(f"{tag}_wl")
    vd = cx.v.S
    sd = cx.s.S
    td = cx.t.S
    st = cx.sem(f"{tag}_st")

    wguT5 = io["wguT"]
    wdT5 = io["wdT5"]

    def SM(par, j, col):
        return sm[:, (par * GT + j) * 16 + col:(par * GT + j) * 16 + col + 1]

    def stage_a(grp):
        par = grp % 2
        t0 = grp * GT
        nxT = nxTB[par]
        for j in range(GT):
            xs = xt[:]
            y.dma(xs, x_in[(t0 + j) * 128:(t0 + j + 1) * 128, :], ld)
            S = lambda col: SM(par, j, col)
            v.op("tensor_reduce", out=S(0), in_=xs, axis=AX, op=Op.add)
            s.op("activation", out=nx[:], in_=xs, func=AF.Square, accum_out=S(1))
            v.op("tensor_scalar", out=S(2), in0=S(0), scalar1=1.0 / D, scalar2=None, op0=Op.mult)
            v.op("tensor_tensor", out=S(4), in0=S(2), in1=S(2), op=Op.mult)
            v.op("tensor_scalar", out=S(3), in0=S(1), scalar1=1.0 / D, scalar2=None, op0=Op.mult)
            vd.inc(v.op("tensor_tensor", out=S(3), in0=S(3), in1=S(4), op=Op.subtract))
            s.op("activation", out=S(4), in_=S(3), func=AF.Sqrt, bias=c["epsb"][:])
            v.op("reciprocal", out=S(4), in_=S(4))
            v.op("tensor_scalar", out=nx[:], in0=xs, scalar1=S(2), scalar2=None, op0=Op.subtract)
            v.op("tensor_reduce", out=S(5), in_=nx[:], axis=AX, op=Op.max,
                 apply_absolute_value=True)
            v.op("tensor_tensor", out=S(6), in0=S(5), in1=S(4), op=Op.mult)
            v.op("tensor_scalar", out=S(6), in0=S(6), scalar1=EPS_Q, scalar2=None, op0=Op.max)
            vd.inc(v.op("reciprocal", out=S(7), in_=S(6)))
            v.op("scalar_tensor_tensor", out=S(10), in0=S(7), scalar=127.0, in1=S(4),
                 op0=Op.mult, op1=Op.mult)
            v.op("tensor_scalar", out=S(7), in0=S(6), scalar1=1.0 / 127.0, scalar2=None, op0=Op.mult)
            v.op("tensor_tensor", out=S(8), in0=S(7), in1=S(7), op=Op.mult)
            v.op("tensor_tensor", out=S(7), in0=S(8), in1=S(7), op=Op.mult)
            vd.inc(v.op("tensor_tensor", out=S(7), in0=S(7), in1=c["gsb"][:], op=Op.mult))
            s.op("activation", out=nx[:], in_=nx[:], func=AF.Identity,
                 bias=c["magicb"][:], scale=S(10))
            s.op("activation", out=nxq[:], in_=nx[:], func=AF.Identity,
                 bias=c["nmagicb"][:])
            for dc in range(NDC):
                t.op("transpose", out=ps_trA[:, dc * 128:(dc + 1) * 128],
                     in_=nxq[:, dc * 128:(dc + 1) * 128], identity=c["ident_b"][:])
            vd.inc(v.op("tensor_copy",
                        out=nxT.ap().rearrange("p (dc t) -> p dc t", dc=NDC)[:, :, j * 128:(j + 1) * 128],
                        in_=ps_trA.ap().rearrange("p (dc t) -> p dc t", dc=NDC)))

    def stage_bcd(grp):
        par = grp % 2
        t0 = grp * GT
        nxT = nxTB[par]
        for sl in range(NSLB):
            wgs = wgsB[sl % 2]
            for kq in range(4):
                y.dma(wgs.ap().rearrange("p (bdc q) -> p bdc q", q=256)[:, kq * 4:(kq + 1) * 4],
                      wguT5[:, sl, kq * 4:(kq + 1) * 4], wl)
            for j in range(GT):
                pg = ps_gu[2 * ((sl * GT + j) % 3)]
                pu = ps_gu[2 * ((sl * GT + j) % 3) + 1]
                for dc in reversed(range(NDC)):
                    t.op("matmul", out=pg[:, 0:256],
                         lhsT=nxT.ap().rearrange("p (dc t) -> p dc t", dc=NDC)[:, dc, j * 128:(j + 1) * 128],
                         rhs=wgs[:, (0 * NDC + dc) * 256:(0 * NDC + dc + 1) * 256],
                         start=(dc == NDC - 1), stop=(dc == 0))
                for dc in reversed(range(NDC)):
                    inst = t.op("matmul", out=pu[:, 0:256],
                                lhsT=nxT.ap().rearrange("p (dc t) -> p dc t", dc=NDC)[:, dc, j * 128:(j + 1) * 128],
                                rhs=wgs[:, (1 * NDC + dc) * 256:(1 * NDC + dc + 1) * 256],
                                start=(dc == NDC - 1), stop=(dc == 0))
                td.inc(inst)
                hs = hbuf[:, j * I_FF + sl * 256:j * I_FF + (sl + 1) * 256]
                s.op("activation", out=hch[:, 0:256], in_=pg[:, 0:256], func=AF.Relu)
                s.op("activation", out=hch[:, 0:256], in_=hch[:, 0:256], func=AF.Square)
                v.op("tensor_tensor", out=hs, in0=hch[:, 0:256], in1=pu[:, 0:256], op=Op.mult)
                S = lambda col: SM(par, j, col)
                v.op("tensor_reduce", out=S(12), in_=hs, axis=AX, op=Op.max,
                     apply_absolute_value=True)
                if sl == 0:
                    vd.inc(v.op("tensor_copy", out=S(8), in_=S(12)))
                else:
                    vd.inc(v.op("tensor_tensor", out=S(8), in0=S(8), in1=S(12), op=Op.max))
        for j in range(GT):
            S = lambda col: SM(par, j, col)
            v.op("tensor_tensor", out=S(9), in0=S(8), in1=S(7), op=Op.mult)
            v.op("tensor_scalar", out=S(9), in0=S(9), scalar1=EPS_Q, scalar2=None, op0=Op.max)
            vd.inc(v.op("reciprocal", out=S(10), in_=S(9)))
            v.op("scalar_tensor_tensor", out=S(10), in0=S(10), scalar=127.0, in1=S(7),
                 op0=Op.mult, op1=Op.mult)
            v.op("scalar_tensor_tensor", out=S(11), in0=S(9), scalar=0.5 / 127.0, in1=c["swdb"][:],
                 op0=Op.mult, op1=Op.mult)
            vd.inc(v.op("tensor_copy", out=S(11), in_=S(11)))
            for sl2 in range(NSL):
                s.op("activation", out=hbuf[:, j * I_FF + sl2 * 512:j * I_FF + (sl2 + 1) * 512],
                     in_=hbuf[:, j * I_FF + sl2 * 512:j * I_FF + (sl2 + 1) * 512],
                     func=AF.Identity, bias=c["magicb"][:], scale=S(10))
                v.op("tensor_scalar", out=nhq[:, (sl2 % 2) * 512:(sl2 % 2 + 1) * 512],
                     in0=hbuf[:, j * I_FF + sl2 * 512:j * I_FF + (sl2 + 1) * 512],
                     scalar1=MAGIC, scalar2=None, op0=Op.subtract)
                for jj in range(4):
                    t.op("transpose", out=ps_tr[:, ((sl2 % 2) * 4 + jj) * 128:((sl2 % 2) * 4 + jj + 1) * 128],
                         in_=nhq[:, ((sl2 % 2) * 4 + jj) * 128:((sl2 % 2) * 4 + jj + 1) * 128],
                         identity=c["ident_b"][:])
                if sl2 % 2 == 1:
                    ic0 = (sl2 - 1) * 4
                    v.op("tensor_copy",
                         out=nhT.ap().rearrange("p (ic t) -> p ic t", ic=NIC)[:, ic0:ic0 + 8, j * 128:(j + 1) * 128],
                         in_=ps_tr.ap().rearrange("p (b t) -> p b t", b=8))
        for dsl in range(4):
            wds = wdsB[dsl % 2]
            for kq in range(4):
                y.dma(wds.ap().rearrange("p (ic q) -> p ic q", q=256)[:, kq * 8:(kq + 1) * 8],
                      wdT5[:, dsl, kq * 8:(kq + 1) * 8], wl)
            for j in range(GT):
                pd = ps_dn[(dsl * GT + j) % 3]
                for ic in reversed(range(NIC)):
                    t.op("matmul", out=pd[:, 0:256],
                         lhsT=nhT.ap().rearrange("p (ic t) -> p ic t", ic=NIC)[:, ic, j * 128:(j + 1) * 128],
                         rhs=wds[:, ic * 256:(ic + 1) * 256],
                         start=(ic == NIC - 1), stop=(ic == 0))
                S = lambda col: SM(par, j, col)
                v.op("tensor_scalar",
                     out=hbuf[:, j * I_FF + dsl * 256:j * I_FF + (dsl + 1) * 256],
                     in0=pd[:, 0:256], scalar1=S(11), scalar2=None, op0=Op.mult)
        for j in range(GT):
            y.dma(xtD[:], x_in[(t0 + j) * 128:(t0 + j + 1) * 128, :], ld)
            v.op("tensor_tensor", out=xtD[:], in0=xtD[:],
                 in1=hbuf[:, j * I_FF:j * I_FF + D], op=Op.add)
            y.dma(x_out[(t0 + j) * 128:(t0 + j + 1) * 128, :], xtD[:], st)

    for grp in range(NG):
        stage_a(grp)
        if grp >= 1:
            stage_bcd(grp - 1)
    stage_bcd(NG - 1)
    g.wait(st, st.n)



def host_consts(params):
    import ml_dtypes
    bf = ml_dtypes.bfloat16
    out = {}
    out["ident_b"] = np.eye(128, dtype=np.float32).astype(bf)
    out["ones_row"] = np.ones((1, 128), np.float32).astype(bf)
    s_i, t_i = np.mgrid[0:64, 0:64]
    out["maskL"] = (t_i > s_i).astype(np.float32).astype(bf)
    out["maskS"] = (t_i >= s_i).astype(np.float32).astype(bf)
    mix = np.zeros((128, 6 * NDC), np.float32)
    for i, nm in enumerate(["mix_r", "mix_w", "mix_k", "mix_v", "mix_a", "mix_g"]):
        m = np.asarray(params[nm]).reshape(D)
        mix[:, i * NDC:(i + 1) * NDC] = m.reshape(NDC, 128).T
    out["mix6"] = mix
    def bc(vec):
        return np.broadcast_to(np.asarray(vec).reshape(1, D), (128, D)).astype(bf)
    out["kkb"] = bc(params["k_k"])
    out["kab"] = bc(params["k_a"])
    out["okab"] = bc(1.0 - np.asarray(params["k_a"]))
    out["rkb"] = bc(np.asarray(params["r_k"]).reshape(D))
    out["gnwb"] = bc(params["gn_weight"])
    out["gnbb"] = bc(params["gn_bias"])
    wB_aug = np.concatenate([np.asarray(params["wB"]), np.asarray(params["w_bias"]).reshape(1, D)], 0)
    aB_aug = np.concatenate([np.asarray(params["aB"]), np.asarray(params["a_bias"]).reshape(1, D)], 0)
    out["wB_aug"] = wB_aug.astype(bf)
    out["aB_aug"] = aB_aug.astype(bf)
    gB = np.asarray(params["gB"])
    gB_c = np.zeros((128, 2 * D), np.float32)
    gB_c[0:128, 0:D] = gB[0:128]
    gB_c[0:32, D:2 * D] = gB[128:160]
    out["gB_c"] = gB_c.astype(bf)
    def fm_chunks(w, cols):
        w = np.asarray(w)
        o = np.zeros((128, NDC * cols), np.float32)
        for dc in range(NDC):
            o[:, dc * cols:(dc + 1) * cols] = w[dc * 128:(dc + 1) * 128, :]
        return o.astype(bf)
    out["wA_c"] = fm_chunks(params["wA"], 64)
    out["aA_c"] = fm_chunks(params["aA"], 64)
    out["gA_c"] = fm_chunks(params["gA"], 160)
    return out



def ph_wcast(cx, io):
    c = cx.bufs
    v, y, g = cx.v, cx.y, cx.g
    wtB = [cx.sbuf(f"wc_t{i}", [128, D], F32, phase=True) for i in range(2)]
    wbB = [cx.sbuf(f"wc_b{i}", [128, D], BF16, phase=True) for i in range(2)]
    wdB = [cx.sbuf(f"wc_d{i}", [128, D], BF16, phase=True) for i in range(2)]
    ld = cx.sem("wc_ld")
    st = cx.sem("wc_st")
    mixcol = {"Wr": 0, "Wk": 2, "Wv": 3}
    for nm in ("Wr", "Wk", "Wv", "Wo"):
        for rr in range(NDC):
            wt, wb, wd_ = wtB[rr % 2], wbB[rr % 2], wdB[rr % 2]
            y.dma(wt[:], io[nm][rr * 128:(rr + 1) * 128, :], ld)
            v.op("tensor_copy", out=wb[:], in_=wt[:])
            y.dma(io[nm + "b"][:, 0:4, rr, :],
                  wb.ap().rearrange("p (nh q) -> p nh q", q=256), st)
            if nm in mixcol:
                mi = mixcol[nm]
                v.op("tensor_scalar", out=wd_[:], in0=wt[:],
                     scalar1=c["mix6"][:, mi * NDC + rr:mi * NDC + rr + 1],
                     scalar2=None, op0=Op.mult)
                y.dma(io[nm + "d"][:, 0:4, rr, :],
                      wd_.ap().rearrange("p (nh q) -> p nh q", q=256), st)
    for src_, dst_, mi, ncol in (("wA_c", "wAd_c", 1, 64), ("aA_c", "aAd_c", 4, 64),
                                 ("gA_c", "gAd_c", 5, 160)):
        for rr in range(NDC):
            v.op("tensor_scalar", out=c[dst_][:, rr * ncol:(rr + 1) * ncol],
                 in0=c[src_][:, rr * ncol:(rr + 1) * ncol],
                 scalar1=c["mix6"][:, mi * NDC + rr:mi * NDC + rr + 1],
                 scalar2=None, op0=Op.mult)



def ph_attn(cx, io, x_in, x_out, T, tag, stages=99):
    c = cx.bufs
    t, v, s, g, y = cx.t, cx.v, cx.s, cx.g, cx.y
    WA = 512
    NW = T // WA
    NT = 4
    NCH = WA // C
    NCP = NCH // 2
    DK = abs(DECAY)

    def sb(name, shape, dtype):
        return cx.sbuf(f"{tag}_{name}", shape, dtype, phase=True)

    xtile = sb("xtile", [128, D], F32)
    xlt = sb("xlt", [128, NDC * (WA + 1)], BF16)
    un1 = sb("un1", [128, NT * D], BF16)
    un2ab = sb("un2ab", [128, 4096], BF16)
    mxs = lambda dc: un2ab[:, dc * WA:dc * WA + WA]
    r_t = sb("r_t", [128, NT * D], BF16)
    k_t = sb("k_t", [128, NT * D], BF16)
    v_t = sb("v_t", [128, NT * D], BF16)
    ag_t = sb("ag_t", [128, NT * D], BF16)
    g_t = sb("g_t", [128, NT * D], BF16)
    kk_t = sb("kk_t", [128, NT * D], BF16)
    b_t = sb("b_t", [128, NT * D], BF16)
    sig = sb("sig", [128, NDC * WA], F32)
    O_t = sig
    cwc = sb("cwc", [128, WA], F32)
    Zt = cwc
    ebuf = sb("ebuf", [128, 3 * WA], F32)
    Rb = ebuf.ap().bitcast(BF16)[:, 0:H * 128]
    base = sb("base", [128, 2 * NCH], F32)
    gam = sb("gam", [128, NDC * NCH], F32)
    ARf = sb("ARf", [128, 2 * NDC * WA], BF16)
    BKf = sb("BKf", [128, 2 * NDC * WA], BF16)
    At_f = ARf.ap()[:, 0:NDC * WA]
    Rt_f = ARf.ap()[:, NDC * WA:2 * NDC * WA]
    Bh_f = BKf.ap()[:, 0:NDC * WA]
    Kh_f = BKf.ap()[:, NDC * WA:2 * NDC * WA]
    BKb_f = sb("BKb_f", [128, 2 * WA], BF16)
    wprojB = [sb(f"wproj{i}", [128, NDC * 256], BF16) for i in range(2)]
    wprojdB = [sb(f"wprojd{i}", [128, NDC * 256], BF16) for i in range(2)]
    lrw = sb("lrw", [65, WA], BF16)
    lra = lrw
    sg128 = sb("sg128", [128, WA], BF16)
    sg32 = sb("sg32", [32, WA], BF16)
    sqs = sb("sqs", [128, D], F32)
    sums = sb("sums", [128, 64], F32)
    LbLk = un2ab
    SqSk = sb("SqSk", [128, H * 128], BF16)
    Xb = sb("Xb", [128, H * 128], BF16)
    Qb = sqs.ap().bitcast(BF16)[:, 0:H * 128]
    Pt = sb("Pt", [128, 8 * 64], BF16)
    Rst = sb("Rst", [128, 8 * 64], BF16)
    Ptb2 = un1.ap()[:, 2048:2560]
    Rstb2 = un1.ap()[:, 2560:3072]
    Zs = sb("Zs", [128, 8 * 64], BF16)
    invn = sb("invn", [128, 16], F32)

    ps_a = cx.psum(f"{tag}_pa", [128, 512], F32)
    ps_b = cx.psum(f"{tag}_pb", [128, 512], F32)
    ps_c = cx.psum(f"{tag}_pc", [128, 512], F32)
    ps_d = cx.psum(f"{tag}_pd", [128, 512], F32)
    ps_e = cx.psum(f"{tag}_pe", [128, 512], F32)
    ps_f = cx.psum(f"{tag}_pf", [128, 512], F32)
    ps_g = cx.psum(f"{tag}_pg", [128, 512], F32)
    ps_bt = cx.psum(f"{tag}_pbt", [128, 1024], BF16)

    ld = cx.sem(f"{tag}_ld")
    wl = cx.sem(f"{tag}_wl")
    st = cx.sem(f"{tag}_st")
    vd, sd, td, gd = v.S, s.S, t.S, g.S

    v.op("memset", ap=lrw[64:65, :], constant=1.0)
    v.op("memset", ap=Zs[:], constant=0.0)
    v.op("memset", ap=xlt[:], constant=0.0)

    WrB, WrD = io["Wrb"], io["Wrd"]
    WkB, WkD = io["Wkb"], io["Wkd"]
    WvB, WvD = io["Wvb"], io["Wvd"]
    WoB = io["Wob"]

    xlc = lambda dc: xlt[:, dc * (WA + 1):dc * (WA + 1) + WA + 1]

    s0 = 0
    def _snap():
        return (cx.t.n_ops, getattr(cx.t, 'n_waits', 0),
                cx.v.n_ops + cx.s.n_ops + cx.g.n_ops + cx.y.n_ops)
    stats = {}
    def mark(nm, s0):
        cur = _snap()
        prev = stats.get(nm, (0, 0, 0))
        stats[nm] = tuple(p + c - z for p, c, z in zip(prev, cur, s0))
        return cur
    cx.attn_stats = stats
    s0 = _snap()
    for w in range(NW):
        tw0 = w * NT
        for dc in range(NDC):
            v.op("tensor_copy", out=xlc(dc)[:, 0:1], in_=xlc(dc)[:, WA:WA + 1])
        for j in range(NT):
            xs = xtile[:]
            y.dma(xs, x_in[(tw0 + j) * 128:(tw0 + j + 1) * 128, :], ld)
            SS = lambda col: sums[:, j * 8 + col:j * 8 + col + 1]
            v.op("tensor_reduce", out=SS(0), in_=xs, axis=AX, op=Op.add)
            s.op("activation", out=sqs[:], in_=xs, func=AF.Square, accum_out=SS(1))
            v.op("tensor_scalar", out=SS(2), in0=SS(0), scalar1=1.0 / D, scalar2=None, op0=Op.mult)
            v.op("tensor_tensor", out=SS(4), in0=SS(2), in1=SS(2), op=Op.mult)
            v.op("tensor_scalar", out=SS(3), in0=SS(1), scalar1=1.0 / D, scalar2=None, op0=Op.mult)
            v.op("tensor_tensor", out=SS(3), in0=SS(3), in1=SS(4), op=Op.subtract)
            s.op("activation", out=SS(4), in_=SS(3), func=AF.Sqrt, bias=c["epsb"][:])
            v.op("reciprocal", out=SS(4), in_=SS(4))
            v.op("tensor_scalar", out=un1[:, j * D:(j + 1) * D], in0=xs,
                 scalar1=SS(2), scalar2=SS(4), op0=Op.subtract, op1=Op.mult)
            for dc0 in range(0, NDC, 8):
                for dd in range(8):
                    t.op("transpose", out=ps_bt[:, dd * 128:(dd + 1) * 128],
                         in_=un1[:, j * D + (dc0 + dd) * 128:j * D + (dc0 + dd + 1) * 128],
                         identity=c["ident_b"][:])
                for dd in range(8):
                    v.op("tensor_copy", out=xlc(dc0 + dd)[:, 1 + j * 128:1 + (j + 1) * 128],
                         in_=ps_bt[:, dd * 128:(dd + 1) * 128])
        if stages < 1:
            continue
        s0 = mark('a1', s0)
        for dc in range(NDC):
            v.op("tensor_tensor", out=un1[:, dc * WA:(dc + 1) * WA],
                 in0=xlc(dc)[:, 0:WA], in1=xlc(dc)[:, 1:WA + 1], op=Op.subtract)

        def proj_1024(wsrc, wdsrc, outbuf, sig_evac=None):
            for nh in range(4):
                wproj = wprojB[nh % 2]
                wprojd = wprojdB[nh % 2]
                for kq in range(2):
                    y.dma(wproj.ap().rearrange("p (dc q) -> p dc q", q=256)[:, kq * 4:(kq + 1) * 4],
                          wsrc[:, nh, kq * 4:(kq + 1) * 4], wl)
                    y.dma(wprojd.ap().rearrange("p (dc q) -> p dc q", q=256)[:, kq * 4:(kq + 1) * 4],
                          wdsrc[:, nh, kq * 4:(kq + 1) * 4], wl)
                for j in range(NT):
                    pp = (ps_c, ps_d)[j % 2]
                    for dc in range(NDC):
                        t.op("matmul", out=pp[:, 0:256],
                             lhsT=xlc(dc)[:, 1 + j * 128:1 + (j + 1) * 128],
                             rhs=wproj[:, dc * 256:(dc + 1) * 256],
                             start=(dc == 0), stop=False)
                    for dc in range(NDC):
                        t.op("matmul", out=pp[:, 0:256],
                             lhsT=un1[:, dc * WA + j * 128:dc * WA + (j + 1) * 128],
                             rhs=wprojd[:, dc * 256:(dc + 1) * 256],
                             start=False, stop=(dc == NDC - 1))
                    v.op("tensor_copy", out=outbuf[:, j * D + nh * 256:j * D + (nh + 1) * 256],
                         in_=pp[:, 0:256])

        def lr_first(wmat, wdmat, ncol, outp, nc_=None):
            nc_ = ncol if nc_ is None else nc_
            for dc in range(NDC):
                t.op("matmul", out=outp,
                     lhsT=wmat[:, dc * ncol:dc * ncol + nc_],
                     rhs=xlc(dc)[:, 1:WA + 1],
                     start=(dc == 0), stop=False)
            for dc in range(NDC):
                t.op("matmul", out=outp,
                     lhsT=wdmat[:, dc * ncol:dc * ncol + nc_],
                     rhs=un1[:, dc * WA:(dc + 1) * WA],
                     start=False, stop=(dc == NDC - 1))

        proj_1024(WrB, WrD, r_t)
        proj_1024(WkB, WkD, k_t)
        proj_1024(WvB, WvD, v_t)
        lr_first(c["aA_c"], c["aAd_c"], 64, ps_e[0:64, 0:512])
        v.op("tensor_copy", out=lra[0:64, :], in_=ps_e[0:64, 0:512])
        for j in range(NT):
            pp = (ps_c, ps_d)[j % 2]
            for nh in range(2):
                t.op("matmul", out=pp[:], lhsT=lra[:, j * 128:(j + 1) * 128],
                     rhs=c["aB_aug"][:, nh * 512:(nh + 1) * 512], start=True, stop=True)
                s.op("activation", out=ag_t[:, j * D + nh * 512:j * D + (nh + 1) * 512],
                     in_=pp[:], func=AF.Sigmoid)
        lr_first(c["wA_c"], c["wAd_c"], 64, ps_e[0:64, 0:512])
        s.op("activation", out=lrw[0:64, :], in_=ps_e[0:64, 0:512], func=AF.Tanh)
        for fc in range(NDC):
            pp = (ps_c, ps_d)[fc % 2]
            t.op("matmul", out=pp[:], lhsT=c["wB_aug"][:, fc * 128:(fc + 1) * 128],
                 rhs=lrw[:], start=True, stop=True)
            s.op("activation", out=sig[:, fc * WA:(fc + 1) * WA], in_=pp[:], func=AF.Sigmoid)
        for part, colrange, outb in ((0, (0, 128), sg128), (1, (128, 160), sg32)):
            nc2 = colrange[1] - colrange[0]
            for dc in range(NDC):
                t.op("matmul", out=ps_e[0:nc2, 0:512],
                     lhsT=c["gA_c"][:, dc * 160 + colrange[0]:dc * 160 + colrange[1]],
                     rhs=xlc(dc)[:, 1:WA + 1],
                     start=(dc == 0), stop=False)
            for dc in range(NDC):
                t.op("matmul", out=ps_e[0:nc2, 0:512],
                     lhsT=c["gAd_c"][:, dc * 160 + colrange[0]:dc * 160 + colrange[1]],
                     rhs=un1[:, dc * WA:(dc + 1) * WA],
                     start=False, stop=(dc == NDC - 1))
            s.op("activation", out=outb[0:nc2, :], in_=ps_e[0:nc2, 0:512], func=AF.Sigmoid)
        for j in range(NT):
            p1 = (ps_c, ps_e)[j % 2]
            p2 = (ps_d, ps_f)[j % 2]
            for nh in range(2):
                t.op("matmul", out=p1[:], lhsT=sg128[:, j * 128:(j + 1) * 128],
                     rhs=c["gB_c"][0:128, nh * 512:(nh + 1) * 512], start=True, stop=True)
                t.op("matmul", out=p2[:], lhsT=sg32[0:32, j * 128:(j + 1) * 128],
                     rhs=c["gB_c"][0:32, D + nh * 512:D + (nh + 1) * 512], start=True, stop=True)
                s.op("copy", out=sqs[:, 0:512], in_=p1[:])
                v.op("tensor_tensor", out=g_t[:, j * D + nh * 512:j * D + (nh + 1) * 512],
                     in0=p2[:], in1=sqs[:, 0:512], op=Op.add)
        for j in range(NT):
            jD = slice(j * D, (j + 1) * D)
            v.op("tensor_tensor", out=kk_t[:, jD], in0=k_t[:, jD], in1=c["kkb"][:], op=Op.mult)
            v.op("tensor_tensor", out=sqs[:], in0=kk_t[:, jD], in1=kk_t[:, jD], op=Op.mult)
            v.op("tensor_reduce", out=invn[:], in_=sqs.ap().rearrange("p (h q) -> p h q", h=H),
                 axis=AX, op=Op.add)
            s.op("activation", out=invn[:], in_=invn[:], func=AF.Sqrt)
            v.op("tensor_scalar", out=invn[:], in0=invn[:], scalar1=1e-12, scalar2=None, op0=Op.max)
            v.op("reciprocal", out=invn[:], in_=invn[:])
            v.op("tensor_tensor", out=kk_t[:, jD], in0=kk_t[:, jD],
                 in1=invn.ap().unsqueeze(2).broadcast_to([128, H, HD]), op=Op.mult)
            v.op("tensor_tensor", out=b_t[:, jD], in0=kk_t[:, jD], in1=ag_t[:, jD], op=Op.mult)
            v.op("tensor_tensor", out=sqs[:], in0=ag_t[:, jD], in1=c["kab"][:], op=Op.mult)
            v.op("tensor_tensor", out=sqs[:], in0=sqs[:], in1=c["okab"][:], op=Op.add)
            v.op("tensor_tensor", out=k_t[:, jD], in0=k_t[:, jD], in1=sqs[:], op=Op.mult)
        if "dbg_rt" in io and w == 0:
            dsem = cx.sem("dbg")
            for nm, buf in (("dbg_rt", r_t), ("dbg_kt", k_t), ("dbg_vt", v_t),
                            ("dbg_ag", ag_t), ("dbg_gt", g_t), ("dbg_kkn", kk_t), ("dbg_bt", b_t)):
                y.dma(io[nm], buf[:, 0:NT * D], dsem)
        if stages < 3:
            continue
        s0 = mark('a3kk', s0)
        def t_fm(src, dst):
            for j in range(NT):
                for dc0 in range(0, NDC, 8):
                    for dd in range(8):
                        t.op("transpose", out=ps_bt[:, dd * 128:(dd + 1) * 128],
                             in_=src[:, j * D + (dc0 + dd) * 128:j * D + (dc0 + dd + 1) * 128],
                             identity=c["ident_b"][:])
                    v.op("tensor_copy",
                         out=apx(dst, 0, 128, dc0 * WA + j * 128, [(WA, 8), (1, 128)]),
                         in_=ps_bt.ap().rearrange("p (dc q) -> p dc q", q=128))
        t_fm(kk_t, At_f)
        t_fm(b_t, Bh_f)
        t_fm(k_t, Kh_f)
        t_fm(r_t, Rt_f)
        if stages < 4:
            continue
        s0 = mark('a4tfm', s0)
        for fc in range(NDC):
            fs = slice(fc * WA, (fc + 1) * WA)
            v.op("tensor_tensor_scan", out=cwc[:], data0=sig[:, fs], data1=sig[:, fs],
                 initial=0.0, op0=Op.add, op1=Op.bypass)
            v.op("tensor_copy", out=base[:, 1:NCH], in_=cwc[:, C - 1:WA - 1:C])
            v.op("memset", ap=base[:, 0:1], constant=0.0)
            v.op("tensor_tensor", out=cwc[:],
                 in0=cwc[:].rearrange("p (ch q) -> p ch q", ch=NCH),
                 in1=base.ap()[:, 0:NCH].unsqueeze(2).broadcast_to([128, NCH, C]), op=Op.subtract)
            v.op("tensor_copy", out=base[:, NCH:2 * NCH], in_=cwc[:, C - 1:WA:C])
            s.op("activation", out=gam[:, fc * NCH:(fc + 1) * NCH],
                 in_=base[:, NCH:2 * NCH], func=AF.Exp, scale=-DK)
            v.op("tensor_tensor", out=ebuf[:, 0:WA], in0=cwc[:], in1=sig[:, fs], op=Op.subtract)
            s.op("activation", out=ebuf[:, 0:WA], in_=ebuf[:, 0:WA], func=AF.Exp, scale=-DK)
            s.op("activation", out=ebuf[:, WA:2 * WA], in_=cwc[:], func=AF.Exp, scale=DK)
            s.op("activation", out=ebuf[:, 2 * WA:3 * WA], in_=cwc[:], func=AF.Exp, scale=-DK)
            v.op("scalar_tensor_tensor", out=At_f[:, fs], in0=At_f[:, fs], scalar=-1.0,
                 in1=ebuf[:, 0:WA], op0=Op.mult, op1=Op.mult)
            v.op("tensor_tensor", out=Rt_f[:, fs], in0=Rt_f[:, fs], in1=ebuf[:, 2 * WA:3 * WA], op=Op.mult)
            v.op("tensor_tensor", out=ebuf[:, 2 * WA:3 * WA],
                 in0=cwc[:].rearrange("p (ch q) -> p ch q", ch=NCH),
                 in1=base.ap()[:, NCH:2 * NCH].unsqueeze(2).broadcast_to([128, NCH, C]),
                 op=Op.subtract)
            s.op("activation", out=ebuf[:, 2 * WA:3 * WA], in_=ebuf[:, 2 * WA:3 * WA],
                 func=AF.Exp, scale=DK)
            v.op("tensor_tensor", out=BKb_f[:, 0:WA], in0=Bh_f[:, fs], in1=ebuf[:, 2 * WA:3 * WA], op=Op.mult)
            v.op("tensor_tensor", out=BKb_f[:, WA:2 * WA], in0=Kh_f[:, fs], in1=ebuf[:, 2 * WA:3 * WA], op=Op.mult)
            v.op("tensor_tensor", out=Bh_f[:, fs], in0=Bh_f[:, fs], in1=ebuf[:, WA:2 * WA], op=Op.mult)
            v.op("tensor_tensor", out=Kh_f[:, fs], in0=Kh_f[:, fs], in1=ebuf[:, WA:2 * WA], op=Op.mult)
            for (srcap, dst) in ((At_f[:, fs], ag_t), (BKb_f[:, 0:WA], kk_t), (BKb_f[:, WA:2 * WA], b_t)):
                for j in range(NT):
                    t.op("transpose", out=ps_bt[:, (j % 8) * 128:(j % 8 + 1) * 128],
                         in_=srcap[:, j * 128:(j + 1) * 128], identity=c["ident_b"][:])
                for j in range(NT):
                    s.op("copy", out=dst[:, j * D + fc * 128:j * D + (fc + 1) * 128],
                         in_=ps_bt[:, (j % 8) * 128:(j % 8 + 1) * 128])
        At_t, Bb_t, Kb_t = ag_t, kk_t, b_t
        if "dbg_sig" in io and w == 0:
            dsem2 = cx.sem("dbg2")
            y.dma(io["dbg_sig"], sig[:, 0:NDC * WA], dsem2)
            y.dma(io["dbg_At"], At_f[:, 0:NDC * WA], dsem2)
            y.dma(io["dbg_Bh"], Bh_f[:, 0:NDC * WA], dsem2)
            y.dma(io["dbg_Att"], At_t[:, 0:NT * D], dsem2)
        if stages < 5:
            continue
        s0 = mark('a56scan', s0)
        for cp in range(NCP):
            c0, c1 = 2 * cp, 2 * cp + 1
            for par in (0, 1):
                hs = par * 64
                for hg in range(4):
                    h1 = par + 4 * hg
                    h2 = h1 + 2
                    pA = (ps_a, ps_c)[hg % 2]
                    pB = (ps_b, ps_d)[hg % 2]
                    for (hh, colb) in ((h1, 0), (h2, 256)):
                        ft = hh // 2
                        for (cc, pp, ocs) in ((c0, pA, 0), (c1, pB, 64)):
                            for (ii, lb, rb) in ((0, Bh_f, At_f), (1, Bh_f, Rt_f),
                                                 (2, Kh_f, At_f), (3, Kh_f, Rt_f)):
                                t.op("matmul", out=pp[ocs:ocs + 64, colb + ii * 64:colb + (ii + 1) * 64],
                                     lhsT=lb[hs:hs + 64, ft * WA + cc * C:ft * WA + (cc + 1) * C],
                                     rhs=rb[hs:hs + 64, ft * WA + cc * C:ft * WA + (cc + 1) * C],
                                     start=True, stop=True, tile_position=(hs, ocs))
                    mL = c["maskL"].ap().unsqueeze(1).unsqueeze(1).broadcast_to([64, 2, 2, 64])
                    mS = c["maskS"].ap().unsqueeze(1).unsqueeze(1).broadcast_to([64, 2, 2, 64])
                    for (pp, ro, co) in ((pA, 0, 0), (pB, 64, 64)):
                        v.op("tensor_tensor",
                             out=apx(LbLk, ro, 64, h1 * 256 + co, [(512, 2), (128, 2), (1, 64)]),
                             in0=apx(pp, ro, 64, 0, [(256, 2), (128, 2), (1, 64)]),
                             in1=mL, op=Op.mult)
                        v.op("tensor_tensor",
                             out=apx(SqSk, ro, 64, h1 * 128, [(256, 2), (64, 2), (1, 64)]),
                             in0=apx(pp, ro, 64, 64, [(256, 2), (128, 2), (1, 64)]),
                             in1=mS, op=Op.mult)
            if stages < 6:
                continue
            v.op("memset", ap=apx(LbLk, 0, 64, 64, [(128, 2 * H), (1, 64)]), constant=0.0)
            v.op("memset", ap=apx(LbLk, 64, 64, 0, [(128, 2 * H), (1, 64)]), constant=0.0)
            s0 = mark('a7cc', s0)
            pbank = [ps_c, ps_d, ps_e, ps_f]
            for q in range(4):
                h0 = 4 * q
                for u in range(4):
                    t.op("matmul", out=pbank[q][:, u * 64:(u + 1) * 64],
                         lhsT=LbLk[:, (h0 + u) * 256 + 128:(h0 + u + 1) * 256],
                         rhs=v_t[:, cp * D + (h0 + u) * 64:cp * D + (h0 + u + 1) * 64],
                         start=True, stop=True)
            for q in range(4):
                h0 = 4 * q
                v.op("tensor_copy", out=apx(Xb, 0, 128, h0 * 128, [(128, 4), (1, 64)]),
                     in_=At_t[:, cp * D + h0 * 64:cp * D + (h0 + 4) * 64])
                v.op("tensor_copy", out=apx(Xb, 0, 128, h0 * 128 + 64, [(128, 4), (1, 64)]),
                     in_=pbank[q][:, 0:256])
            for q in range(4):
                h0 = 4 * q
                for u in range(4):
                    t.op("transpose", out=ps_bt[:, ((q % 2) * 4 + u) * 128:((q % 2) * 4 + u + 1) * 128],
                         in_=LbLk[:, (h0 + u) * 256:(h0 + u) * 256 + 128],
                         identity=c["ident_b"][:])
                if q % 2 == 1:
                    v.op("tensor_copy", out=Rb[:, (q - 1) * 512:(q + 1) * 512],
                         in_=ps_bt[:, 0:1024])
            for lev in range(6):
                if lev == 0:
                    qsl = lambda hh: LbLk[:, hh * 256:hh * 256 + 128]
                else:
                    qsl = lambda hh: Qb[:, hh * 128:(hh + 1) * 128]
                for q in range(4):
                    h0 = 4 * q
                    for u in range(4):
                        t.op("matmul", out=pbank[q][:, u * 128:(u + 1) * 128],
                             lhsT=qsl(h0 + u),
                             rhs=Xb[:, (h0 + u) * 128:(h0 + u + 1) * 128],
                             start=True, stop=True)
                for q in range(4):
                    blk4 = slice(4 * q * 128, 4 * (q + 1) * 128)
                    v.op("tensor_tensor", out=Xb[:, blk4], in0=pbank[q][:, 0:512],
                         in1=Xb[:, blk4], op=Op.add)
                if lev < 5:
                    for q in range(4):
                        h0 = 4 * q
                        for u in range(4):
                            t.op("matmul", out=pbank[q][:, u * 128:(u + 1) * 128],
                                 lhsT=Rb[:, (h0 + u) * 128:(h0 + u + 1) * 128],
                                 rhs=qsl(h0 + u),
                                 start=True, stop=True)
                    if lev < 4:
                        for qq in range(4):
                            h0 = 4 * qq
                            for u in range(4):
                                t.op("matmul", out=ps_g[:, u * 128:(u + 1) * 128],
                                     lhsT=qsl(h0 + u),
                                     rhs=Rb[:, (h0 + u) * 128:(h0 + u + 1) * 128],
                                     start=True, stop=True)
                            s.op("copy", out=Rb[:, h0 * 128:(h0 + 4) * 128],
                                 in_=ps_g[:, 0:512])
                    for q in range(4):
                        s.op("copy", out=Qb[:, 4 * q * 128:4 * (q + 1) * 128],
                             in_=pbank[q][:, 0:512])
            Pt2 = {c0: ps_c, c1: ps_f}
            Gp2 = {c0: ps_d, c1: ps_g}
            Rp2 = {c0: ps_e, c1: ps_b}
            PtS = {c0: Pt[:, 0:512] if False else Pt, c1: Pt}
            for cc in (c0, c1):
                cs = (cc % 2) * 64
                tok = lambda buf, hh: buf[cs:cs + 64, cp * D + hh * 64:cp * D + (hh + 1) * 64]
                Xsl = lambda hh, a2, b2: Xb[cs:cs + 64, hh * 128 + a2:hh * 128 + b2]
                for hh in range(H):
                    hs2 = (hh % 2) * 64
                    hcol = slice((hh // 2) * 64, (hh // 2 + 1) * 64)
                    t.op("matmul", out=Pt2[cc][hs2:hs2 + 64, hcol], lhsT=Xsl(hh, 0, 64),
                         rhs=tok(Bb_t, hh), start=True, stop=True, tile_position=(cs, hs2))
                    t.op("matmul", out=Gp2[cc][hs2:hs2 + 64, hcol], lhsT=tok(Bb_t, hh),
                         rhs=Xsl(hh, 64, 128), start=True, stop=False, tile_position=(cs, hs2))
                    t.op("matmul", out=Gp2[cc][hs2:hs2 + 64, hcol], lhsT=tok(Kb_t, hh),
                         rhs=tok(v_t, hh), start=False, stop=True, tile_position=(cs, hs2))
                    t.op("matmul", out=Rp2[cc][hs2:hs2 + 64, hcol], lhsT=Xsl(hh, 0, 64),
                         rhs=SqSk[cs:cs + 64, hh * 128:hh * 128 + 64],
                         start=True, stop=True, tile_position=(cs, hs2))
            PtB = (Pt, Ptb2)
            RsB = (Rst, Rstb2)
            for ci, cc in enumerate((c0, c1)):
                s.op("copy", out=PtB[ci][:], in_=Pt2[cc][:])
                v.op("tensor_tensor", out=RsB[ci][:], in0=Rp2[cc][:],
                     in1=apx(Rt_f, 0, 128, cc * C, [(WA, 8), (1, 64)]), op=Op.add)
            for ci, cc in enumerate((c0, c1)):
                cs = (cc % 2) * 64
                tok = lambda buf, hh: buf[cs:cs + 64, cp * D + hh * 64:cp * D + (hh + 1) * 64]
                Xsl = lambda hh, a2, b2: Xb[cs:cs + 64, hh * 128 + a2:hh * 128 + b2]
                pRZlo, pRZhi = ps_a, ps_f
                pOlo, pOhi = ps_c, ps_e
                for hh in range(H):
                    hs2 = (hh % 2) * 64
                    pRZ = pRZlo if hh < 8 else pRZhi
                    t.op("matmul", out=pRZ[cs:cs + 64, (hh % 8) * 64:(hh % 8 + 1) * 64],
                         lhsT=RsB[ci][hs2:hs2 + 64, (hh // 2) * 64:(hh // 2 + 1) * 64],
                         rhs=Zs[hs2:hs2 + 64, (hh // 2) * 64:(hh // 2 + 1) * 64],
                         start=True, stop=True, tile_position=(hs2, cs))
                for hh in range(H):
                    pO = pOlo if hh < 8 else pOhi
                    t.op("matmul", out=pO[cs:cs + 64, (hh % 8) * 64:(hh % 8 + 1) * 64],
                         lhsT=SqSk[cs:cs + 64, hh * 128:hh * 128 + 64], rhs=Xsl(hh, 64, 128),
                         start=True, stop=False, tile_position=(cs, cs))
                    t.op("matmul", out=pO[cs:cs + 64, (hh % 8) * 64:(hh % 8 + 1) * 64],
                         lhsT=SqSk[cs:cs + 64, hh * 128 + 64:(hh + 1) * 128], rhs=tok(v_t, hh),
                         start=False, stop=True, tile_position=(cs, cs))
                s.op("copy", out=sqs[0:64, 0:512], in_=pRZlo[cs:cs + 64, 0:512])
                s.op("copy", out=sqs[0:64, 512:1024], in_=pRZhi[cs:cs + 64, 0:512])
                otile = (cc // 2)
                orow = O_t[:, otile * D:(otile + 1) * D]
                v.op("tensor_tensor", out=orow[cs:cs + 64, 0:512], in0=pOlo[cs:cs + 64, 0:512],
                     in1=sqs[0:64, 0:512], op=Op.add)
                v.op("tensor_tensor", out=orow[cs:cs + 64, 512:1024], in0=pOhi[cs:cs + 64, 0:512],
                     in1=sqs[0:64, 512:1024], op=Op.add)
                for hh in range(H):
                    hs2 = (hh % 2) * 64
                    hcol = slice((hh // 2) * 64, (hh // 2 + 1) * 64)
                    chp0 = ps_b if cc == c0 else ps_f
                    t.op("matmul", out=chp0[hs2:hs2 + 64, hcol],
                         lhsT=PtB[ci][hs2:hs2 + 64, hcol],
                         rhs=Zs[hs2:hs2 + 64, hcol], start=True, stop=True,
                         tile_position=(hs2, hs2))
                chp = ps_b if cc == c0 else ps_f
                v.op("tensor_tensor", out=Zt[:], in0=Zs[:],
                     in1=apx(gam, 0, 128, cc, [(NCH, 8), (0, 64)]), op=Op.mult)
                v.op("tensor_tensor", out=Zt[:], in0=chp[:], in1=Zt[:], op=Op.add)
                v.op("tensor_tensor", out=Zs[:], in0=Gp2[cc][:], in1=Zt[:], op=Op.add)
        for j in range(NT):
            jD = slice(j * D, (j + 1) * D)
            ot = O_t[:, jD]
            GS = lambda a: sums[:, 32 + a * 16:32 + (a + 1) * 16]
            v.op("tensor_reduce", out=GS(0), in_=ot.rearrange("p (h q) -> p h q", h=H),
                 axis=AX, op=Op.add)
            v.op("tensor_tensor", out=sqs[:], in0=ot, in1=ot, op=Op.mult)
            v.op("tensor_reduce", out=GS(1), in_=sqs.ap().rearrange("p (h q) -> p h q", h=H),
                 axis=AX, op=Op.add)
            v.op("tensor_scalar", out=GS(0), in0=GS(0), scalar1=1.0 / HD, scalar2=None, op0=Op.mult)
            v.op("tensor_scalar", out=GS(1), in0=GS(1), scalar1=1.0 / HD, scalar2=None, op0=Op.mult)
            v.op("tensor_tensor", out=sums[:, 0:16], in0=GS(0), in1=GS(0), op=Op.mult)
            v.op("tensor_tensor", out=GS(1), in0=GS(1), in1=sums[:, 0:16], op=Op.subtract)
            s.op("activation", out=GS(1), in_=GS(1), func=AF.Sqrt, bias=c["gnepsb"][:])
            v.op("reciprocal", out=GS(1), in_=GS(1))
            v.op("tensor_tensor", out=ot, in0=ot,
                 in1=GS(0).unsqueeze(2).broadcast_to([128, H, HD]), op=Op.subtract)
            v.op("tensor_tensor", out=ot, in0=ot,
                 in1=GS(1).unsqueeze(2).broadcast_to([128, H, HD]), op=Op.mult)
            v.op("tensor_tensor", out=ot, in0=ot, in1=c["gnwb"][:], op=Op.mult)
            v.op("tensor_tensor", out=ot, in0=ot, in1=c["gnbb"][:], op=Op.add)
            v.op("tensor_tensor", out=sqs[:], in0=r_t[:, jD], in1=k_t[:, jD], op=Op.mult)
            v.op("tensor_tensor", out=sqs[:], in0=sqs[:], in1=c["rkb"][:], op=Op.mult)
            v.op("tensor_reduce", out=sums[:, 0:16], in_=sqs.ap().rearrange("p (h q) -> p h q", h=H),
                 axis=AX, op=Op.add)
            v.op("tensor_tensor", out=sqs[:], in0=v_t[:, jD],
                 in1=sums.ap()[:, 0:16].unsqueeze(2).broadcast_to([128, H, HD]), op=Op.mult)
            v.op("tensor_tensor", out=ot, in0=ot, in1=sqs[:], op=Op.add)
            v.op("tensor_tensor", out=un1[:, jD], in0=ot, in1=g_t[:, jD], op=Op.mult)
            for dc0 in range(0, NDC, 8):
                for dd in range(8):
                    t.op("transpose", out=ps_bt[:, dd * 128:(dd + 1) * 128],
                         in_=un1[:, j * D + (dc0 + dd) * 128:j * D + (dc0 + dd + 1) * 128],
                         identity=c["ident_b"][:])
                v.op("tensor_copy",
                     out=Rb.ap().rearrange("p (dc q) -> p dc q", q=128)[:, dc0:dc0 + 8, 0:128]
                     if False else apx(Rb, 0, 128, dc0 * 128, [(128, 8), (1, 128)]),
                     in_=ps_bt[:, 0:1024])
            y.dma(xtile[:], x_in[(tw0 + j) * 128:(tw0 + j + 1) * 128, :], ld)
            if j == 0:
                for nh in range(4):
                    y.dma((wprojB + wprojdB)[nh].ap().rearrange("p (dc q) -> p dc q", q=256),
                          WoB[:, nh], wl)
            for nh in range(4):
                wproj = (wprojB + wprojdB)[nh % 4]
                pp = (ps_c, ps_d)[nh % 2]
                for dc in range(NDC):
                    t.op("matmul", out=pp[:, 0:256], lhsT=Rb[:, dc * 128:(dc + 1) * 128],
                         rhs=wproj[:, dc * 256:(dc + 1) * 256],
                         start=(dc == 0), stop=(dc == NDC - 1))
                v.op("tensor_tensor", out=xtile[:, nh * 256:(nh + 1) * 256],
                     in0=pp[:, 0:256], in1=xtile[:, nh * 256:(nh + 1) * 256], op=Op.add)
            y.dma(x_out[(tw0 + j) * 128:(tw0 + j + 1) * 128, :], xtile[:], st)
        s0 = mark('a10gn', s0)



def build_module(nc, io, T, dev=False, wait_targets=None):
    cx = Ctx(nc, dev=dev)
    cx.wait_targets = wait_targets
    cx.engines()
    blk = cx.es.enter_context(nc.Block())
    ph_consts(cx, io)
    with cx.phase("wq"):
        ph_wq(cx, io, mid=lambda: ph_wcast(cx, io))
    with cx.phase("ffn1"):
        ph_ffn(cx, io, io["x"], io["x1"], T, "f1")
    with cx.phase("attn"):
        ph_attn(cx, io, io["x1"], io["x2"], T, "a1")
    with cx.phase("ffn2"):
        ph_ffn(cx, io, io["x2"], io["out"], T, "f2")
    cx.fence()
    cx.es.close()
    return cx


def declare_io(nc, T, params_shapes=None):
    io = {}
    io["x"] = nc.dram_tensor("x", [T, D], F32, kind="ExternalInput").ap()
    io["wg"] = nc.dram_tensor("wg", [I_FF, D], F32, kind="ExternalInput").ap()
    io["wu"] = nc.dram_tensor("wu", [I_FF, D], F32, kind="ExternalInput").ap()
    io["wd"] = nc.dram_tensor("wd", [D, I_FF], F32, kind="ExternalInput").ap()
    for nm in ("Wr", "Wk", "Wv", "Wo"):
        io[nm] = nc.dram_tensor(nm, [D, D], F32, kind="ExternalInput").ap()
        io[nm + "b"] = nc.dram_tensor(nm + "b", [128, 4, NDC, 256], BF16, kind="Internal").ap()
        if nm != "Wo":
            io[nm + "d"] = nc.dram_tensor(nm + "d", [128, 4, NDC, 256], BF16, kind="Internal").ap()
    io["wguT"] = nc.dram_tensor("wguT", [128, I_FF // 256, 16, 256], BF16, kind="Internal").ap()
    io["wdT5"] = nc.dram_tensor("wdT5", [128, 4, NIC, 256], BF16, kind="Internal").ap()
    io["x1"] = nc.dram_tensor("x1", [T, D], F32, kind="Internal").ap()
    io["x2"] = nc.dram_tensor("x2", [T, D], F32, kind="Internal").ap()
    io["out"] = nc.dram_tensor("out", [T, D], F32, kind="ExternalOutput").ap()
    return io


CONST_NAMES_BF = ["ident_b", "ones_row", "kkb", "kab", "okab", "rkb", "gnwb", "gnbb",
                  "wB_aug", "aB_aug", "gB_c", "wA_c", "aA_c", "gA_c", "maskL", "maskS"]


def declare_const_io(nc, io, hc):
    import ml_dtypes
    for nm, arr in hc.items():
        dt = BF16 if arr.dtype == ml_dtypes.bfloat16 else F32
        io[nm] = nc.dram_tensor(f"c_{nm}", list(arr.shape), dt, kind="ExternalInput").ap()



from concourse.bass_utils import run_bass_kernel_spmd

T_FULL = 4096


def kernel(**inputs):
    x_all = np.ascontiguousarray(np.asarray(inputs["L0_attn_hidden_states"], dtype=np.float32))
    B = x_all.shape[0]
    assert B == 8 and x_all.shape[1] == T_FULL and x_all.shape[2] == D

    pmap = {
        "mix_r": "L0_attn_mix_r", "mix_w": "L0_attn_mix_w", "mix_k": "L0_attn_mix_k",
        "mix_v": "L0_attn_mix_v", "mix_a": "L0_attn_mix_a", "mix_g": "L0_attn_mix_g",
        "k_k": "L0_attn_k_k", "k_a": "L0_attn_k_a", "r_k": "L0_attn_r_k",
        "gn_weight": "L0_attn_gn_weight", "gn_bias": "L0_attn_gn_bias",
        "wA": "L0_attn_wA", "wB": "L0_attn_wB", "w_bias": "L0_attn_w_bias",
        "aA": "L0_attn_aA", "aB": "L0_attn_aB", "a_bias": "L0_attn_a_bias",
        "gA": "L0_attn_gA", "gB": "L0_attn_gB",
    }
    params = {k: np.asarray(inputs[v], dtype=np.float32) for k, v in pmap.items()}
    hc = host_consts(params)

    nc0 = bass.Bass("TRN2", target_bir_lowering=False, debug=False,
                    detect_race_conditions=False)
    io0 = declare_io(nc0, T_FULL)
    declare_const_io(nc0, io0, hc)
    cx0 = build_module(nc0, io0, T_FULL, dev=False)
    wt = {k: set(v) for k, v in cx0.rec_targets.items()}

    nc = bass.Bass("TRN2", target_bir_lowering=False, debug=False,
                   detect_race_conditions=False)
    io = declare_io(nc, T_FULL)
    declare_const_io(nc, io, hc)
    build_module(nc, io, T_FULL, dev=False, wait_targets=wt)

    shared = dict(
        wg=np.asarray(inputs["L0_ffn_w_gate"], dtype=np.float32),
        wu=np.asarray(inputs["L0_ffn_w_up"], dtype=np.float32),
        wd=np.asarray(inputs["L0_ffn_w_down"], dtype=np.float32),
        Wr=np.asarray(inputs["L0_attn_Wr"], dtype=np.float32),
        Wk=np.asarray(inputs["L0_attn_Wk"], dtype=np.float32),
        Wv=np.asarray(inputs["L0_attn_Wv"], dtype=np.float32),
        Wo=np.asarray(inputs["L0_attn_Wo"], dtype=np.float32),
        **{f"c_{k}": v for k, v in hc.items()},
    )
    in_maps = [dict(x=x_all[b], **shared) for b in range(B)]
    res = run_bass_kernel_spmd(nc, in_maps, core_ids=list(range(B)))
    out = np.stack([np.asarray(res.results[b]["out"]) for b in range(B)], axis=0)
    return out.astype(np.float32)


# version 11 (final)
# speedup vs baseline: 1.1033x; 1.1033x over previous
import contextlib
import numpy as np
import concourse.bass as bass
import concourse.mybir as mybir
from concourse.alu_op_type import AluOpType as Op

F32 = mybir.dt.float32
F32R = mybir.dt.float32r
BF16 = mybir.dt.bfloat16
AF = mybir.ActivationFunctionType
AX = mybir.AxisListType.X
MAGIC = float(3 * 2**22)
DECAY = -0.6065306597126334
EPS_Q = 1e-5
LN_EPS = 1e-5
GN_EPS = 64 * 1e-5

D = 1024
I_FF = 4096
H = 16
HD = 64
C = 64
NDC = D // 128
NIC = I_FF // 128


class Sem:
    def __init__(self, nc, es, name):
        self.h = es.enter_context(nc.semaphore(name))
        self.n = 0

    def inc(self, inst, k=16):
        inst.then_inc(self.h, k)
        self.n += k
        return inst


READ_KEYS = ("in_", "in0", "in1", "lhsT", "rhs", "identity", "scalar", "scalar1",
             "scalar2", "bias", "scale", "data0", "data1", "initial", "mask",
             "on_true", "on_false", "weights", "in_max", "in_values")
WRITE_KEYS = ("out", "accum_out", "ap", "dest", "out_max", "out_indices")


def _ap_region(ap):
    t = ap.tensor
    shape = t.shape
    if type(t).__name__ == "PSumTensorHandle":
        free = 1
        for s_ in shape[1:]:
            free *= s_
        p_lo, p_hi = 0, shape[0]
        dims = list(ap.ap)
        off = ap.offset
        if isinstance(off, int) and dims and free > 0:
            step0, cnt0 = dims[0]
            if step0 == free:
                p_lo = off // free
                p_hi = p_lo + cnt0
        return ((t.name, "ps"), p_lo, p_hi, 0, free)
    free = 1
    for s in shape[1:]:
        free *= s
    dims = list(ap.ap)
    off = ap.offset
    if not isinstance(off, int):
        off = 0
        return (t.name, 0, shape[0], 0, free)
    p_lo, p_hi, f_lo, f_hi = 0, shape[0], 0, free
    if dims and free > 0:
        step0, cnt0 = dims[0]
        if step0 == free or (step0 == 0 and cnt0 == 1) or len(shape) == 1:
            p_lo = off // free if free else 0
            p_hi = p_lo + cnt0
            rest = dims[1:]
            f_lo = off % free if free else 0
            span = 1
            for st, cn in rest:
                span += (cn - 1) * abs(st)
            f_hi = f_lo + span
    return (t.name, p_lo, p_hi, f_lo, f_hi)


def _overlap(a, b):
    if a[0] != b[0]:
        return False
    if isinstance(a[0], tuple):
        return True
    return a[1] < b[2] and b[1] < a[2] and a[3] < b[4] and b[3] < a[4]


class Dep:
    __slots__ = ("sem", "n", "region", "dma", "src")
    def __init__(self, sem, n, region, dma=None, src=None):
        self.sem = sem
        self.n = n
        self.region = region
        self.dma = dma
        self.src = src

    def wait_val(self):
        return self.dma.n if self.dma is not None else self.n

    def cap(self):
        return self.dma.n if self.dma is not None else (self.src.n_all if self.src else None)


class Eng:
    def __init__(self, ctx, name):
        self.ctx = ctx
        self.name = name
        self.e = getattr(ctx.nc, name)
        self.self_sem = ctx.es.enter_context(ctx.nc.semaphore(f"sem_{name}"))
        self.n_all = 0
        self.n_marked = 0
        self.n_ops = 0
        self.waited = {}
        self.S = self
        tg = ctx.wait_targets
        if tg is not None:
            self.marked_sorted = sorted(tg.get(name, set()))
            self.marked_set = set(self.marked_sorted)
        else:
            self.marked_sorted = []
            self.marked_set = set()
        ctx.eng_by_sem[id(self.self_sem)] = self

    @property
    def h(self):
        return self.self_sem

    @property
    def n(self):
        return self.n_all

    def inc(self, inst, k=1):
        return inst

    def _wait(self, sem, val, cap=None):
        key = id(sem)
        tgt = self.ctx.wait_targets
        if key in self.ctx.eng_by_sem:
            if tgt is None:
                self.ctx.rec_targets.setdefault(
                    self.ctx.eng_by_sem[key].name, set()).add(val)
            else:
                peng = self.ctx.eng_by_sem[key]
                marked = peng.marked_sorted
                import bisect
                i = bisect.bisect_left(marked, val)
                assert i < len(marked) and marked[i] == val, (
                    self.name, peng.name, val, marked[max(0, i - 2):i + 2])
                val = i + 1
        if self.waited.get(key, -1) >= val:
            return
        self.e.wait_ge(sem, val)
        self.waited[key] = val
        self.n_ops += 1
        self.n_waits = getattr(self, "n_waits", 0) + 1

    def wait(self, sem, val=None):
        self._wait(sem.h, sem.n if val is None else val)

    def _sync_deps(self, reads, writes):
        tr = self.ctx.track
        need = {}
        skip_self = self.name == "tensor" and not getattr(self.ctx, "no_tskip", False)
        def _need(d, my_region=None):
            if skip_self and d.src is self:
                if (my_region is None or
                        (my_region[1] == d.region[1] and my_region[2] == d.region[2])):
                    return
            key = id(d.sem)
            v, cp_ = d.wait_val(), d.cap()
            if key not in need or need[key][0] < v:
                need[key] = (v, cp_, d.sem)
        for r in reads:
            for d in tr.get(r[0], {}).get("w", []):
                if _overlap(r, d.region):
                    _need(d, r)
        for w in writes:
            ent = tr.get(w[0], {})
            for d in ent.get("w", []):
                if _overlap(w, d.region):
                    _need(d, w)
            for d in ent.get("r", []):
                if _overlap(w, d.region):
                    _need(d, w)
        for v, cp_, semh in need.values():
            self._wait(semh, v, cp_)

    @staticmethod
    def _prune(lst, key):
        best = {}
        for d in lst:
            k = id(d.sem)
            if k not in best or best[k].n < d.n:
                best[k] = d
        whole = (key, 0, 1 << 30, 0, 1 << 30)
        return [Dep(d.sem, d.n, whole, d.dma, d.src) for d in best.values()]

    def _record(self, reads, writes, sem, n, dma=None, src_eng=None):
        tr = self.ctx.track
        for r in reads:
            ent = tr.setdefault(r[0], {"w": [], "r": []})
            ent["r"].append(Dep(sem, n, r, dma, src_eng))
            if len(ent["r"]) > 24:
                ent["r"] = self._prune(ent["r"], r[0])
        for w in writes:
            ent = tr.setdefault(w[0], {"w": [], "r": []})
            ent["w"].append(Dep(sem, n, w, dma, src_eng))
            if len(ent["w"]) > 24:
                ent["w"] = self._prune(ent["w"], w[0])

    def op(self, fname, *args, **kwargs):
        assert not args, "use kwargs only"
        reads, writes = [], []
        rowgrp = None
        if fname in ("matmul", "transpose"):
            lap = kwargs.get("lhsT", kwargs.get("in_"))
            lr = _ap_region(lap)
            rowgrp = (lr[1], lr[2])
        for k, vv in kwargs.items():
            if isinstance(vv, bass.AP):
                if k in WRITE_KEYS:
                    reg = _ap_region(vv)
                    if rowgrp is not None and isinstance(reg[0], tuple):
                        reg = (reg[0], rowgrp[0], rowgrp[1], reg[3], reg[4])
                    writes.append(reg)
                elif k in READ_KEYS:
                    reads.append(_ap_region(vv))
        self._sync_deps(reads, writes)
        inst = getattr(self.e, fname)(**kwargs)
        self.n_all += 1
        if self.ctx.wait_targets is None or self.n_all in self.marked_set:
            inst.then_inc(self.self_sem, 1)
            self.n_marked += 1
        self.n_ops += 1
        self._record(reads, writes, self.self_sem, self.n_all, src_eng=self)
        return inst

    def dma(self, out, in_, sem=None, k=16, **kw):
        pool = self.ctx.dma_pool
        sem = pool[self.ctx.dma_rr % len(pool)]
        self.ctx.dma_rr += 1
        reads = [_ap_region(in_)]
        writes = [_ap_region(out)]
        self._sync_deps(reads, writes)
        inst = self.e.dma_start(out=out, in_=in_, **kw)
        sem.inc(inst, k)
        self.n_ops += 1
        self._record(reads, writes, sem.h, sem.n, dma=sem)
        return inst


class Ctx:
    def __init__(self, nc, dev=True):
        self.nc = nc
        self.dev = dev
        self.es = contextlib.ExitStack()
        self.ps_es = None
        self.phase_es = None
        self.bufs = {}
        self.dma_sems = []
        self.track = {}
        self.wait_hist = {}
        self._uid = 0
        self.dma_pool = [Sem(nc, self.es, f"dmap{i}") for i in range(32)]
        self.dma_rr = 0
        self.wait_targets = None
        self.rec_targets = {}
        self.eng_by_sem = {}

    def uid(self):
        self._uid += 1
        return self._uid

    def sem(self, name):
        s = Sem(self.nc, self.es, f"{name}_{self.uid()}")
        self.dma_sems.append(s)
        return s

    def sbuf(self, name, shape, dtype, phase=False):
        es = self.phase_es if phase else self.es
        t = es.enter_context(self.nc.sbuf_tensor(f"{name}", list(shape), dtype))
        self.bufs[name] = t
        return t

    def psum(self, name, shape, dtype=F32, phase=True):
        es = self.phase_es if phase else self.es
        t = es.enter_context(self.nc.psum_tensor(f"{name}", list(shape), dtype))
        self.bufs[name] = t
        return t

    @contextlib.contextmanager
    def phase(self, name):
        self.phase_es = contextlib.ExitStack()
        try:
            yield
        finally:
            self.fence()
            self.phase_es.close()
            self.phase_es = None

    def fence(self):
        engs = [self.t, self.v, self.s, self.g, self.y]
        for e in engs:
            for o in engs:
                if o is not e and (o.S.n if True else 0) > 0:
                    e.wait(o.S)
            for sem in self.dma_sems + self.dma_pool:
                if sem.n > 0:
                    e.wait(sem)

    def warm(self):
        ident = self.bufs.get("ident_b")
        if ident is not None:
            self.t.e.ldweights(ident[:, 0:64])
            self.t.n_ops += 1

    def engines(self):
        self.t = Eng(self, "tensor")
        self.v = Eng(self, "vector")
        self.s = Eng(self, "scalar")
        self.g = Eng(self, "gpsimd")
        self.y = Eng(self, "sync")
        return self


def apx(buf, p0, np_, col0, dims):
    if isinstance(buf, bass.AP):
        h = buf.tensor
        base = buf.offset
    else:
        h = buf
        base = 0
    fs = 1
    for s_ in h.shape[1:]:
        fs *= s_
    return bass.AP(h, base + p0 * fs + col0, [[fs, np_]] + [list(d) for d in dims])



def ph_consts(cx, io):
    c = cx.bufs
    names_bf = [("ident_b", [128, 128]), ("ones_row", [1, 128]),
                ("kkb", [128, D]), ("kab", [128, D]), ("okab", [128, D]),
                ("rkb", [128, D]), ("gnwb", [128, D]), ("gnbb", [128, D]),
                ("wB_aug", [65, D]), ("aB_aug", [65, D]), ("gB_c", [128, 2 * D]),
                ("wA_c", [128, NDC * 64]), ("aA_c", [128, NDC * 64]),
                ("gA_c", [128, NDC * 160]),
                ("maskL", [64, 64]), ("maskS", [64, 64])]
    ld = cx.sem("c_ld")
    for nm, shp in names_bf:
        cx.sbuf(nm, shp, BF16)
        cx.y.dma(c[nm][:], io[nm], ld)
    cx.sbuf("mix6", [128, 6 * NDC], F32)
    cx.y.dma(c["mix6"][:], io["mix6"], ld)
    cx.sbuf("wAd_c", [128, NDC * 64], BF16)
    cx.sbuf("aAd_c", [128, NDC * 64], BF16)
    cx.sbuf("gAd_c", [128, NDC * 160], BF16)
    cx.sbuf("gsb", [128, 1], F32)
    cx.sbuf("swdb", [128, 1], F32)
    magic = cx.sbuf("magicb", [128, 1], F32)
    cx.v.op("memset", ap=magic[:], constant=MAGIC)
    nmagic = cx.sbuf("nmagicb", [128, 1], F32)
    cx.v.op("memset", ap=nmagic[:], constant=-MAGIC)
    epsb = cx.sbuf("epsb", [128, 1], F32)
    cx.v.op("memset", ap=epsb[:], constant=LN_EPS)
    gnepsb = cx.sbuf("gnepsb", [128, 1], F32)
    cx.v.op("memset", ap=gnepsb[:], constant=GN_EPS)
    cx.c_ld = ld



def ph_wq(cx, io, mid=None):
    c = cx.bufs
    t, v, s, g, y = cx.t, cx.v, cx.s, cx.g, cx.y
    specs = [
        (io["wg"], I_FF, D, None, 0),
        (io["wu"], I_FF, D, None, 1),
        (io["wd"], D, I_FF, None, 2),
    ]
    wtileB = [cx.sbuf(f"wq_tile{i}", [128, I_FF], F32, phase=True) for i in range(2)]
    wqbB = [cx.sbuf(f"wq_qb{i}", [128, I_FF], BF16, phase=True) for i in range(2)]
    stg2B = [cx.sbuf(f"wq_stg2{i}", [128, 2 * I_FF], BF16, phase=True) for i in range(2)]
    acc = cx.sbuf("wq_acc", [128, 4], F32, phase=True)
    red = cx.sbuf("wq_red", [128, 1], F32, phase=True)
    sc = cx.sbuf("wq_sc", [128, 4], F32, phase=True)
    scal = cx.sbuf("wq_scal", [1, 8], F32, phase=True)
    onesb = cx.sbuf("ones_col", [128, 1], F32, phase=True)
    onesr_f = cx.sbuf("ones_row_f", [1, 128], F32, phase=True)
    ps_t = cx.psum("wq_ps", [128, 1024], BF16)
    ps_t2 = cx.psum("wq_ps2", [128, 1024], BF16)
    ps_s = cx.psum("wq_ss", [128, 16], F32)

    ld = cx.sem("wq_ld")
    vd = cx.v.S
    sd = cx.s.S
    td = cx.t.S
    st = cx.sem("wq_st")

    v.wait(cx.c_ld)
    v.op("memset", ap=onesb[:], constant=1.0)
    v.op("memset", ap=onesr_f[:], constant=1.0)
    v.op("memset", ap=acc[:], constant=0.0)

    pi = 0
    for (src, R, Cc, dst, mi) in specs:
        for r in range(R // 128):
            wtile = wtileB[pi % 2]
            pi += 1
            y.dma(wtile[:, 0:Cc], src[r * 128:(r + 1) * 128, :], ld)
            v.op("tensor_reduce", out=red[:], in_=wtile[:, 0:Cc], axis=AX, op=Op.add,
                 apply_absolute_value=True)
            vd.inc(v.op("tensor_tensor", out=acc[:, mi:mi + 1], in0=acc[:, mi:mi + 1],
                        in1=red[:], op=Op.add))
    if mid is not None:
        mid()
    t.wait(vd)
    td.inc(t.op("matmul", out=ps_s[0:1, 0:4], lhsT=onesb[:], rhs=acc[:, 0:4],
                start=True, stop=True))
    v.wait(td)
    vd.inc(v.op("tensor_scalar", out=scal[0:1, 0:4], in0=ps_s[0:1, 0:4],
                scalar1=1.0 / 4194304.0, scalar2=EPS_Q, op0=Op.mult, op1=Op.max))
    vd.inc(v.op("reciprocal", out=scal[0:1, 4:8], in_=scal[0:1, 0:4]))
    vd.inc(v.op("tensor_tensor", out=scal[0:1, 3:4], in0=scal[0:1, 0:1],
                in1=scal[0:1, 0:1], op=Op.mult))
    vd.inc(v.op("tensor_tensor", out=scal[0:1, 3:4], in0=scal[0:1, 3:4],
                in1=scal[0:1, 1:2], op=Op.mult))
    t.wait(vd)
    td.inc(t.op("matmul", out=ps_s[:, 8:16], lhsT=onesr_f[0:1, 0:128],
                rhs=scal[0:1, 0:8], start=True, stop=True))
    v.wait(td)
    vd.inc(v.op("tensor_copy", out=sc[:, 0:4], in_=ps_s[:, 12:16]))
    vd.inc(v.op("tensor_copy", out=c["gsb"][:], in_=ps_s[:, 11:12]))
    vd.inc(v.op("tensor_copy", out=c["swdb"][:], in_=ps_s[:, 10:11]))

    if "dbg" in io:
        g.wait(vd, vd.n)
        y.dma(io["dbg"][0:128, 0:4], sc[:, 0:4], st)
        y.dma(io["dbg"][0:1, 4:12], scal[0:1, 0:8], st)
        y.dma(io["dbg"][0:128, 12:16], acc[:, 0:4], st)
    for (src, R, Cc, dst, mi) in specs:
        nct = Cc // 128
        for r in range(R // 128):
            wtile, wqb = wtileB[pi % 2], wqbB[pi % 2]
            stg2 = stg2B[(r // 2) % 2]
            pst = (ps_t, ps_t2)[pi % 2]
            pi += 1
            y.dma(wtile[:, 0:Cc], src[r * 128:(r + 1) * 128, :], ld)
            s.op("activation", out=wtile[:, 0:Cc], in_=wtile[:, 0:Cc],
                 func=AF.Identity, bias=c["magicb"][:], scale=sc[:, mi:mi + 1])
            s.op("activation", out=wtile[:, 0:Cc], in_=wtile[:, 0:Cc],
                 func=AF.Identity, bias=c["nmagicb"][:])
            v.op("tensor_scalar", out=wqb[:, 0:Cc], in0=wtile[:, 0:Cc],
                 scalar1=1.0, scalar2=-1.0, op0=Op.min, op1=Op.max)
            for cb0 in range(0, nct, 8):
                nb = min(8, nct - cb0)
                for j in range(nb):
                    t.op("transpose", out=pst[:, j * 128:(j + 1) * 128],
                         in_=wqb[:, (cb0 + j) * 128:(cb0 + j + 1) * 128],
                         identity=c["ident_b"][:])
                v.op("tensor_copy",
                     out=apx(stg2, 0, 128, cb0 * 256 + (r % 2) * 128, [(256, nb), (1, 128)]),
                     in_=pst.ap().rearrange("p (b q) -> p b q", q=128)[:, 0:nb])
            if r % 2 == 1:
                if mi < 2:
                    d5 = io["wguT"][:, r // 2, mi * 8:mi * 8 + 8, :]
                else:
                    d5 = io["wdT5"][:, r // 2, 0:NIC, :]
                y.dma(d5, stg2.ap().rearrange("p (cb q) -> p cb q", q=256)[:, 0:nct], st)
    cx.wq_st = st



def ph_ffn(cx, io, x_in, x_out, T, tag):
    c = cx.bufs
    t, v, s, g, y = cx.t, cx.v, cx.s, cx.g, cx.y
    GT = 4
    NT = T // 128
    NG = NT // GT
    NSL = I_FF // 512
    NSLB = I_FF // 256

    xt = cx.sbuf(f"{tag}_x", [128, D], F32, phase=True)
    xtD = cx.sbuf(f"{tag}_xD", [128, D], F32, phase=True)
    nxTB = [cx.sbuf(f"{tag}_nxT{i}", [128, GT * D], BF16, phase=True) for i in range(2)]
    hbuf = cx.sbuf(f"{tag}_h", [128, GT * I_FF], F32, phase=True)
    nhT = cx.sbuf(f"{tag}_nhT", [128, GT * I_FF // 32 * 32], BF16, phase=True)
    wgsB = [cx.sbuf(f"{tag}_wgs{i}", [128, 2 * NDC * 256], BF16, phase=True) for i in range(2)]
    nhq = cx.sbuf(f"{tag}_nhq", [128, 1024], BF16, phase=True)
    wdsB = [cx.sbuf(f"{tag}_wds{i}", [128, NIC * 256], BF16, phase=True) for i in range(2)]
    nx = cx.sbuf(f"{tag}_nx", [128, D], F32, phase=True)
    nxq = cx.sbuf(f"{tag}_nxq", [128, D], BF16, phase=True)
    hch = cx.sbuf(f"{tag}_hch", [128, 256], F32, phase=True)
    sm = cx.sbuf(f"{tag}_sm", [128, 16 * GT * 2], F32, phase=True)
    ps_gu = [cx.psum(f"{tag}_gu{i}", [128, 512], F32) for i in range(6)]
    ps_tr = cx.psum(f"{tag}_tr", [128, 1024], BF16)
    ps_trA = cx.psum(f"{tag}_trA", [128, 1024], BF16)
    ps_dn = ps_gu

    ld = cx.sem(f"{tag}_ld")
    wl = cx.sem(f"{tag}_wl")
    vd = cx.v.S
    sd = cx.s.S
    td = cx.t.S
    st = cx.sem(f"{tag}_st")

    wguT5 = io["wguT"]
    wdT5 = io["wdT5"]

    def SM(par, j, col):
        return sm[:, (par * GT + j) * 16 + col:(par * GT + j) * 16 + col + 1]

    def stage_a(grp):
        par = grp % 2
        t0 = grp * GT
        nxT = nxTB[par]
        for j in range(GT):
            xs = xt[:]
            y.dma(xs, x_in[(t0 + j) * 128:(t0 + j + 1) * 128, :], ld)
            S = lambda col: SM(par, j, col)
            v.op("tensor_reduce", out=S(0), in_=xs, axis=AX, op=Op.add)
            s.op("activation", out=nx[:], in_=xs, func=AF.Square, accum_out=S(1))
            v.op("tensor_scalar", out=S(2), in0=S(0), scalar1=1.0 / D, scalar2=None, op0=Op.mult)
            v.op("tensor_tensor", out=S(4), in0=S(2), in1=S(2), op=Op.mult)
            v.op("tensor_scalar", out=S(3), in0=S(1), scalar1=1.0 / D, scalar2=None, op0=Op.mult)
            vd.inc(v.op("tensor_tensor", out=S(3), in0=S(3), in1=S(4), op=Op.subtract))
            s.op("activation", out=S(4), in_=S(3), func=AF.Sqrt, bias=c["epsb"][:])
            v.op("reciprocal", out=S(4), in_=S(4))
            v.op("tensor_scalar", out=nx[:], in0=xs, scalar1=S(2), scalar2=None, op0=Op.subtract)
            v.op("tensor_reduce", out=S(5), in_=nx[:], axis=AX, op=Op.max,
                 apply_absolute_value=True)
            v.op("tensor_tensor", out=S(6), in0=S(5), in1=S(4), op=Op.mult)
            v.op("tensor_scalar", out=S(6), in0=S(6), scalar1=EPS_Q, scalar2=None, op0=Op.max)
            vd.inc(v.op("reciprocal", out=S(7), in_=S(6)))
            v.op("scalar_tensor_tensor", out=S(10), in0=S(7), scalar=127.0, in1=S(4),
                 op0=Op.mult, op1=Op.mult)
            v.op("tensor_scalar", out=S(7), in0=S(6), scalar1=1.0 / 127.0, scalar2=None, op0=Op.mult)
            v.op("tensor_tensor", out=S(8), in0=S(7), in1=S(7), op=Op.mult)
            v.op("tensor_tensor", out=S(7), in0=S(8), in1=S(7), op=Op.mult)
            vd.inc(v.op("tensor_tensor", out=S(7), in0=S(7), in1=c["gsb"][:], op=Op.mult))
            s.op("activation", out=nx[:], in_=nx[:], func=AF.Identity,
                 bias=c["magicb"][:], scale=S(10))
            s.op("activation", out=nxq[:], in_=nx[:], func=AF.Identity,
                 bias=c["nmagicb"][:])
            for dc in range(NDC):
                t.op("transpose", out=ps_trA[:, dc * 128:(dc + 1) * 128],
                     in_=nxq[:, dc * 128:(dc + 1) * 128], identity=c["ident_b"][:])
            vd.inc(v.op("tensor_copy",
                        out=nxT.ap().rearrange("p (dc t) -> p dc t", dc=NDC)[:, :, j * 128:(j + 1) * 128],
                        in_=ps_trA.ap().rearrange("p (dc t) -> p dc t", dc=NDC)))

    def stage_bcd(grp):
        par = grp % 2
        t0 = grp * GT
        nxT = nxTB[par]
        for sl in range(NSLB):
            wgs = wgsB[sl % 2]
            y.dma(wgs.ap().rearrange("p (bdc q) -> p bdc q", q=256),
                  wguT5[:, sl], wl)
            for j in range(GT):
                pg = ps_gu[2 * ((sl * GT + j) % 3)]
                pu = ps_gu[2 * ((sl * GT + j) % 3) + 1]
                for dc in reversed(range(NDC)):
                    t.op("matmul", out=pg[:, 0:256],
                         lhsT=nxT.ap().rearrange("p (dc t) -> p dc t", dc=NDC)[:, dc, j * 128:(j + 1) * 128],
                         rhs=wgs[:, (0 * NDC + dc) * 256:(0 * NDC + dc + 1) * 256],
                         start=(dc == NDC - 1), stop=(dc == 0))
                for dc in reversed(range(NDC)):
                    inst = t.op("matmul", out=pu[:, 0:256],
                                lhsT=nxT.ap().rearrange("p (dc t) -> p dc t", dc=NDC)[:, dc, j * 128:(j + 1) * 128],
                                rhs=wgs[:, (1 * NDC + dc) * 256:(1 * NDC + dc + 1) * 256],
                                start=(dc == NDC - 1), stop=(dc == 0))
                td.inc(inst)
                hs = hbuf[:, j * I_FF + sl * 256:j * I_FF + (sl + 1) * 256]
                s.op("activation", out=hch[:, 0:256], in_=pg[:, 0:256], func=AF.Relu)
                s.op("activation", out=hch[:, 0:256], in_=hch[:, 0:256], func=AF.Square)
                v.op("tensor_tensor", out=hs, in0=hch[:, 0:256], in1=pu[:, 0:256], op=Op.mult)
                S = lambda col: SM(par, j, col)
                v.op("tensor_reduce", out=S(12), in_=hs, axis=AX, op=Op.max,
                     apply_absolute_value=True)
                if sl == 0:
                    vd.inc(v.op("tensor_copy", out=S(8), in_=S(12)))
                else:
                    vd.inc(v.op("tensor_tensor", out=S(8), in0=S(8), in1=S(12), op=Op.max))
        for j in range(GT):
            S = lambda col: SM(par, j, col)
            v.op("tensor_tensor", out=S(9), in0=S(8), in1=S(7), op=Op.mult)
            v.op("tensor_scalar", out=S(9), in0=S(9), scalar1=EPS_Q, scalar2=None, op0=Op.max)
            vd.inc(v.op("reciprocal", out=S(10), in_=S(9)))
            v.op("scalar_tensor_tensor", out=S(10), in0=S(10), scalar=127.0, in1=S(7),
                 op0=Op.mult, op1=Op.mult)
            v.op("scalar_tensor_tensor", out=S(11), in0=S(9), scalar=0.5 / 127.0, in1=c["swdb"][:],
                 op0=Op.mult, op1=Op.mult)
            vd.inc(v.op("tensor_copy", out=S(11), in_=S(11)))
            for sl2 in range(NSL):
                s.op("activation", out=hbuf[:, j * I_FF + sl2 * 512:j * I_FF + (sl2 + 1) * 512],
                     in_=hbuf[:, j * I_FF + sl2 * 512:j * I_FF + (sl2 + 1) * 512],
                     func=AF.Identity, bias=c["magicb"][:], scale=S(10))
                v.op("tensor_scalar", out=nhq[:, (sl2 % 2) * 512:(sl2 % 2 + 1) * 512],
                     in0=hbuf[:, j * I_FF + sl2 * 512:j * I_FF + (sl2 + 1) * 512],
                     scalar1=MAGIC, scalar2=None, op0=Op.subtract)
                for jj in range(4):
                    t.op("transpose", out=ps_tr[:, ((sl2 % 2) * 4 + jj) * 128:((sl2 % 2) * 4 + jj + 1) * 128],
                         in_=nhq[:, ((sl2 % 2) * 4 + jj) * 128:((sl2 % 2) * 4 + jj + 1) * 128],
                         identity=c["ident_b"][:])
                if sl2 % 2 == 1:
                    ic0 = (sl2 - 1) * 4
                    v.op("tensor_copy",
                         out=nhT.ap().rearrange("p (ic t) -> p ic t", ic=NIC)[:, ic0:ic0 + 8, j * 128:(j + 1) * 128],
                         in_=ps_tr.ap().rearrange("p (b t) -> p b t", b=8))
        for dsl in range(4):
            wds = wdsB[dsl % 2]
            y.dma(wds.ap().rearrange("p (ic q) -> p ic q", q=256),
                  wdT5[:, dsl], wl)
            for j in range(GT):
                pd = ps_dn[(dsl * GT + j) % 3]
                for ic in reversed(range(NIC)):
                    t.op("matmul", out=pd[:, 0:256],
                         lhsT=nhT.ap().rearrange("p (ic t) -> p ic t", ic=NIC)[:, ic, j * 128:(j + 1) * 128],
                         rhs=wds[:, ic * 256:(ic + 1) * 256],
                         start=(ic == NIC - 1), stop=(ic == 0))
                S = lambda col: SM(par, j, col)
                v.op("tensor_scalar",
                     out=hbuf[:, j * I_FF + dsl * 256:j * I_FF + (dsl + 1) * 256],
                     in0=pd[:, 0:256], scalar1=S(11), scalar2=None, op0=Op.mult)
        for j in range(GT):
            y.dma(xtD[:], x_in[(t0 + j) * 128:(t0 + j + 1) * 128, :], ld)
            v.op("tensor_tensor", out=xtD[:], in0=xtD[:],
                 in1=hbuf[:, j * I_FF:j * I_FF + D], op=Op.add)
            y.dma(x_out[(t0 + j) * 128:(t0 + j + 1) * 128, :], xtD[:], st)

    for grp in range(NG):
        stage_a(grp)
        if grp >= 1:
            stage_bcd(grp - 1)
    stage_bcd(NG - 1)
    g.wait(st, st.n)



def host_consts(params):
    import ml_dtypes
    bf = ml_dtypes.bfloat16
    out = {}
    out["ident_b"] = np.eye(128, dtype=np.float32).astype(bf)
    out["ones_row"] = np.ones((1, 128), np.float32).astype(bf)
    s_i, t_i = np.mgrid[0:64, 0:64]
    out["maskL"] = (t_i > s_i).astype(np.float32).astype(bf)
    out["maskS"] = (t_i >= s_i).astype(np.float32).astype(bf)
    mix = np.zeros((128, 6 * NDC), np.float32)
    for i, nm in enumerate(["mix_r", "mix_w", "mix_k", "mix_v", "mix_a", "mix_g"]):
        m = np.asarray(params[nm]).reshape(D)
        mix[:, i * NDC:(i + 1) * NDC] = m.reshape(NDC, 128).T
    out["mix6"] = mix
    def bc(vec):
        return np.broadcast_to(np.asarray(vec).reshape(1, D), (128, D)).astype(bf)
    out["kkb"] = bc(params["k_k"])
    out["kab"] = bc(params["k_a"])
    out["okab"] = bc(1.0 - np.asarray(params["k_a"]))
    out["rkb"] = bc(np.asarray(params["r_k"]).reshape(D))
    out["gnwb"] = bc(params["gn_weight"])
    out["gnbb"] = bc(params["gn_bias"])
    wB_aug = np.concatenate([np.asarray(params["wB"]), np.asarray(params["w_bias"]).reshape(1, D)], 0)
    aB_aug = np.concatenate([np.asarray(params["aB"]), np.asarray(params["a_bias"]).reshape(1, D)], 0)
    out["wB_aug"] = wB_aug.astype(bf)
    out["aB_aug"] = aB_aug.astype(bf)
    gB = np.asarray(params["gB"])
    gB_c = np.zeros((128, 2 * D), np.float32)
    gB_c[0:128, 0:D] = gB[0:128]
    gB_c[0:32, D:2 * D] = gB[128:160]
    out["gB_c"] = gB_c.astype(bf)
    def fm_chunks(w, cols):
        w = np.asarray(w)
        o = np.zeros((128, NDC * cols), np.float32)
        for dc in range(NDC):
            o[:, dc * cols:(dc + 1) * cols] = w[dc * 128:(dc + 1) * 128, :]
        return o.astype(bf)
    out["wA_c"] = fm_chunks(params["wA"], 64)
    out["aA_c"] = fm_chunks(params["aA"], 64)
    out["gA_c"] = fm_chunks(params["gA"], 160)
    return out



def ph_wcast(cx, io):
    c = cx.bufs
    v, y, g = cx.v, cx.y, cx.g
    wtB = [cx.sbuf(f"wc_t{i}", [128, D], F32, phase=True) for i in range(2)]
    wbB = [cx.sbuf(f"wc_b{i}", [128, D], BF16, phase=True) for i in range(2)]
    wdB = [cx.sbuf(f"wc_d{i}", [128, D], BF16, phase=True) for i in range(2)]
    ld = cx.sem("wc_ld")
    st = cx.sem("wc_st")
    mixcol = {"Wr": 0, "Wk": 2, "Wv": 3}
    for nm in ("Wr", "Wk", "Wv", "Wo"):
        for rr in range(NDC):
            wt, wb, wd_ = wtB[rr % 2], wbB[rr % 2], wdB[rr % 2]
            y.dma(wt[:], io[nm][rr * 128:(rr + 1) * 128, :], ld)
            v.op("tensor_copy", out=wb[:], in_=wt[:])
            y.dma(io[nm + "b"][:, 0:4, rr, :],
                  wb.ap().rearrange("p (nh q) -> p nh q", q=256), st)
            if nm in mixcol:
                mi = mixcol[nm]
                v.op("tensor_scalar", out=wd_[:], in0=wt[:],
                     scalar1=c["mix6"][:, mi * NDC + rr:mi * NDC + rr + 1],
                     scalar2=None, op0=Op.mult)
                y.dma(io[nm + "d"][:, 0:4, rr, :],
                      wd_.ap().rearrange("p (nh q) -> p nh q", q=256), st)
    for src_, dst_, mi, ncol in (("wA_c", "wAd_c", 1, 64), ("aA_c", "aAd_c", 4, 64),
                                 ("gA_c", "gAd_c", 5, 160)):
        for rr in range(NDC):
            v.op("tensor_scalar", out=c[dst_][:, rr * ncol:(rr + 1) * ncol],
                 in0=c[src_][:, rr * ncol:(rr + 1) * ncol],
                 scalar1=c["mix6"][:, mi * NDC + rr:mi * NDC + rr + 1],
                 scalar2=None, op0=Op.mult)



def ph_attn(cx, io, x_in, x_out, T, tag, stages=99):
    c = cx.bufs
    t, v, s, g, y = cx.t, cx.v, cx.s, cx.g, cx.y
    WA = 512
    NW = T // WA
    NT = 4
    NCH = WA // C
    NCP = NCH // 2
    DK = abs(DECAY)

    def sb(name, shape, dtype):
        return cx.sbuf(f"{tag}_{name}", shape, dtype, phase=True)

    xtile = sb("xtile", [128, D], F32)
    xlt = sb("xlt", [128, NDC * (WA + 1)], BF16)
    un1 = sb("un1", [128, NT * D], BF16)
    un2ab = sb("un2ab", [128, 4096], BF16)
    mxs = lambda dc: un2ab[:, dc * WA:dc * WA + WA]
    r_t = sb("r_t", [128, NT * D], BF16)
    k_t = sb("k_t", [128, NT * D], BF16)
    v_t = sb("v_t", [128, NT * D], BF16)
    ag_t = sb("ag_t", [128, NT * D], BF16)
    g_t = sb("g_t", [128, NT * D], BF16)
    kk_t = sb("kk_t", [128, NT * D], BF16)
    b_t = sb("b_t", [128, NT * D], BF16)
    sig = sb("sig", [128, NDC * WA], F32)
    O_t = sig
    cwc = sb("cwc", [128, WA], F32)
    Zt = cwc
    ebuf = sb("ebuf", [128, 3 * WA], F32)
    Rb = ebuf.ap().bitcast(BF16)[:, 0:H * 128]
    base = sb("base", [128, 2 * NCH], F32)
    gam = sb("gam", [128, NDC * NCH], F32)
    ARf = sb("ARf", [128, 2 * NDC * WA], BF16)
    BKf = sb("BKf", [128, 2 * NDC * WA], BF16)
    At_f = ARf.ap()[:, 0:NDC * WA]
    Rt_f = ARf.ap()[:, NDC * WA:2 * NDC * WA]
    Bh_f = BKf.ap()[:, 0:NDC * WA]
    Kh_f = BKf.ap()[:, NDC * WA:2 * NDC * WA]
    BKb_f = sb("BKb_f", [128, 2 * WA], BF16)
    wprojB = [sb(f"wproj{i}", [128, NDC * 256], BF16) for i in range(2)]
    wprojdB = [sb(f"wprojd{i}", [128, NDC * 256], BF16) for i in range(2)]
    lrw = sb("lrw", [65, WA], BF16)
    lra = lrw
    sg128 = sb("sg128", [128, WA], BF16)
    sg32 = sb("sg32", [32, WA], BF16)
    sqs = sb("sqs", [128, D], F32)
    sums = sb("sums", [128, 64], F32)
    LbLk = un2ab
    SqSk = sb("SqSk", [128, H * 128], BF16)
    Xb = sb("Xb", [128, H * 128], BF16)
    Qb = sqs.ap().bitcast(BF16)[:, 0:H * 128]
    Pt = sb("Pt", [128, 8 * 64], BF16)
    Rst = sb("Rst", [128, 8 * 64], BF16)
    Ptb2 = un1.ap()[:, 2048:2560]
    Rstb2 = un1.ap()[:, 2560:3072]
    Zs = sb("Zs", [128, 8 * 64], BF16)
    invn = sb("invn", [128, 16], F32)

    ps_a = cx.psum(f"{tag}_pa", [128, 512], F32)
    ps_b = cx.psum(f"{tag}_pb", [128, 512], F32)
    ps_c = cx.psum(f"{tag}_pc", [128, 512], F32)
    ps_d = cx.psum(f"{tag}_pd", [128, 512], F32)
    ps_e = cx.psum(f"{tag}_pe", [128, 512], F32)
    ps_f = cx.psum(f"{tag}_pf", [128, 512], F32)
    ps_g = cx.psum(f"{tag}_pg", [128, 512], F32)
    ps_bt = cx.psum(f"{tag}_pbt", [128, 1024], BF16)

    ld = cx.sem(f"{tag}_ld")
    wl = cx.sem(f"{tag}_wl")
    st = cx.sem(f"{tag}_st")
    vd, sd, td, gd = v.S, s.S, t.S, g.S

    v.op("memset", ap=lrw[64:65, :], constant=1.0)
    v.op("memset", ap=Zs[:], constant=0.0)
    v.op("memset", ap=xlt[:], constant=0.0)

    WrB, WrD = io["Wrb"], io["Wrd"]
    WkB, WkD = io["Wkb"], io["Wkd"]
    WvB, WvD = io["Wvb"], io["Wvd"]
    WoB = io["Wob"]

    xlc = lambda dc: xlt[:, dc * (WA + 1):dc * (WA + 1) + WA + 1]

    s0 = 0
    def _snap():
        return (cx.t.n_ops, getattr(cx.t, 'n_waits', 0),
                cx.v.n_ops + cx.s.n_ops + cx.g.n_ops + cx.y.n_ops)
    stats = {}
    def mark(nm, s0):
        cur = _snap()
        prev = stats.get(nm, (0, 0, 0))
        stats[nm] = tuple(p + c - z for p, c, z in zip(prev, cur, s0))
        return cur
    cx.attn_stats = stats
    s0 = _snap()
    for w in range(NW):
        tw0 = w * NT
        for dc in range(NDC):
            v.op("tensor_copy", out=xlc(dc)[:, 0:1], in_=xlc(dc)[:, WA:WA + 1])
        for j in range(NT):
            xs = xtile[:]
            y.dma(xs, x_in[(tw0 + j) * 128:(tw0 + j + 1) * 128, :], ld)
            SS = lambda col: sums[:, j * 8 + col:j * 8 + col + 1]
            v.op("tensor_reduce", out=SS(0), in_=xs, axis=AX, op=Op.add)
            s.op("activation", out=sqs[:], in_=xs, func=AF.Square, accum_out=SS(1))
            v.op("tensor_scalar", out=SS(2), in0=SS(0), scalar1=1.0 / D, scalar2=None, op0=Op.mult)
            v.op("tensor_tensor", out=SS(4), in0=SS(2), in1=SS(2), op=Op.mult)
            v.op("tensor_scalar", out=SS(3), in0=SS(1), scalar1=1.0 / D, scalar2=None, op0=Op.mult)
            v.op("tensor_tensor", out=SS(3), in0=SS(3), in1=SS(4), op=Op.subtract)
            s.op("activation", out=SS(4), in_=SS(3), func=AF.Sqrt, bias=c["epsb"][:])
            v.op("reciprocal", out=SS(4), in_=SS(4))
            v.op("tensor_scalar", out=un1[:, j * D:(j + 1) * D], in0=xs,
                 scalar1=SS(2), scalar2=SS(4), op0=Op.subtract, op1=Op.mult)
            for dc0 in range(0, NDC, 8):
                for dd in range(8):
                    t.op("transpose", out=ps_bt[:, dd * 128:(dd + 1) * 128],
                         in_=un1[:, j * D + (dc0 + dd) * 128:j * D + (dc0 + dd + 1) * 128],
                         identity=c["ident_b"][:])
                for dd in range(8):
                    v.op("tensor_copy", out=xlc(dc0 + dd)[:, 1 + j * 128:1 + (j + 1) * 128],
                         in_=ps_bt[:, dd * 128:(dd + 1) * 128])
        if stages < 1:
            continue
        s0 = mark('a1', s0)
        for dc in range(NDC):
            v.op("tensor_tensor", out=un1[:, dc * WA:(dc + 1) * WA],
                 in0=xlc(dc)[:, 0:WA], in1=xlc(dc)[:, 1:WA + 1], op=Op.subtract)

        def proj_1024(wsrc, wdsrc, outbuf, sig_evac=None):
            for nh in range(4):
                wproj = wprojB[nh % 2]
                wprojd = wprojdB[nh % 2]
                y.dma(wproj.ap().rearrange("p (dc q) -> p dc q", q=256),
                      wsrc[:, nh], wl)
                y.dma(wprojd.ap().rearrange("p (dc q) -> p dc q", q=256),
                      wdsrc[:, nh], wl)
                for j in range(NT):
                    pp = (ps_c, ps_d)[j % 2]
                    for dc in range(NDC):
                        t.op("matmul", out=pp[:, 0:256],
                             lhsT=xlc(dc)[:, 1 + j * 128:1 + (j + 1) * 128],
                             rhs=wproj[:, dc * 256:(dc + 1) * 256],
                             start=(dc == 0), stop=False)
                    for dc in range(NDC):
                        t.op("matmul", out=pp[:, 0:256],
                             lhsT=un1[:, dc * WA + j * 128:dc * WA + (j + 1) * 128],
                             rhs=wprojd[:, dc * 256:(dc + 1) * 256],
                             start=False, stop=(dc == NDC - 1))
                    v.op("tensor_copy", out=outbuf[:, j * D + nh * 256:j * D + (nh + 1) * 256],
                         in_=pp[:, 0:256])

        def lr_first(wmat, wdmat, ncol, outp, nc_=None):
            nc_ = ncol if nc_ is None else nc_
            for dc in range(NDC):
                t.op("matmul", out=outp,
                     lhsT=wmat[:, dc * ncol:dc * ncol + nc_],
                     rhs=xlc(dc)[:, 1:WA + 1],
                     start=(dc == 0), stop=False)
            for dc in range(NDC):
                t.op("matmul", out=outp,
                     lhsT=wdmat[:, dc * ncol:dc * ncol + nc_],
                     rhs=un1[:, dc * WA:(dc + 1) * WA],
                     start=False, stop=(dc == NDC - 1))

        proj_1024(WrB, WrD, r_t)
        proj_1024(WkB, WkD, k_t)
        proj_1024(WvB, WvD, v_t)
        lr_first(c["aA_c"], c["aAd_c"], 64, ps_e[0:64, 0:512])
        v.op("tensor_copy", out=lra[0:64, :], in_=ps_e[0:64, 0:512])
        for j in range(NT):
            pp = (ps_c, ps_d)[j % 2]
            for nh in range(2):
                t.op("matmul", out=pp[:], lhsT=lra[:, j * 128:(j + 1) * 128],
                     rhs=c["aB_aug"][:, nh * 512:(nh + 1) * 512], start=True, stop=True)
                s.op("activation", out=ag_t[:, j * D + nh * 512:j * D + (nh + 1) * 512],
                     in_=pp[:], func=AF.Sigmoid)
        lr_first(c["wA_c"], c["wAd_c"], 64, ps_e[0:64, 0:512])
        s.op("activation", out=lrw[0:64, :], in_=ps_e[0:64, 0:512], func=AF.Tanh)
        for fc in range(NDC):
            pp = (ps_c, ps_d)[fc % 2]
            t.op("matmul", out=pp[:], lhsT=c["wB_aug"][:, fc * 128:(fc + 1) * 128],
                 rhs=lrw[:], start=True, stop=True)
            s.op("activation", out=sig[:, fc * WA:(fc + 1) * WA], in_=pp[:], func=AF.Sigmoid)
        for part, colrange, outb in ((0, (0, 128), sg128), (1, (128, 160), sg32)):
            nc2 = colrange[1] - colrange[0]
            for dc in range(NDC):
                t.op("matmul", out=ps_e[0:nc2, 0:512],
                     lhsT=c["gA_c"][:, dc * 160 + colrange[0]:dc * 160 + colrange[1]],
                     rhs=xlc(dc)[:, 1:WA + 1],
                     start=(dc == 0), stop=False)
            for dc in range(NDC):
                t.op("matmul", out=ps_e[0:nc2, 0:512],
                     lhsT=c["gAd_c"][:, dc * 160 + colrange[0]:dc * 160 + colrange[1]],
                     rhs=un1[:, dc * WA:(dc + 1) * WA],
                     start=False, stop=(dc == NDC - 1))
            s.op("activation", out=outb[0:nc2, :], in_=ps_e[0:nc2, 0:512], func=AF.Sigmoid)
        for j in range(NT):
            p1 = (ps_c, ps_e)[j % 2]
            p2 = (ps_d, ps_f)[j % 2]
            for nh in range(2):
                t.op("matmul", out=p1[:], lhsT=sg128[:, j * 128:(j + 1) * 128],
                     rhs=c["gB_c"][0:128, nh * 512:(nh + 1) * 512], start=True, stop=True)
                t.op("matmul", out=p2[:], lhsT=sg32[0:32, j * 128:(j + 1) * 128],
                     rhs=c["gB_c"][0:32, D + nh * 512:D + (nh + 1) * 512], start=True, stop=True)
                s.op("copy", out=sqs[:, 0:512], in_=p1[:])
                v.op("tensor_tensor", out=g_t[:, j * D + nh * 512:j * D + (nh + 1) * 512],
                     in0=p2[:], in1=sqs[:, 0:512], op=Op.add)
        for j in range(NT):
            jD = slice(j * D, (j + 1) * D)
            v.op("tensor_tensor", out=kk_t[:, jD], in0=k_t[:, jD], in1=c["kkb"][:], op=Op.mult)
            v.op("tensor_tensor", out=sqs[:], in0=kk_t[:, jD], in1=kk_t[:, jD], op=Op.mult)
            v.op("tensor_reduce", out=invn[:], in_=sqs.ap().rearrange("p (h q) -> p h q", h=H),
                 axis=AX, op=Op.add)
            s.op("activation", out=invn[:], in_=invn[:], func=AF.Sqrt)
            v.op("tensor_scalar", out=invn[:], in0=invn[:], scalar1=1e-12, scalar2=None, op0=Op.max)
            v.op("reciprocal", out=invn[:], in_=invn[:])
            v.op("tensor_tensor", out=kk_t[:, jD], in0=kk_t[:, jD],
                 in1=invn.ap().unsqueeze(2).broadcast_to([128, H, HD]), op=Op.mult)
            v.op("tensor_tensor", out=b_t[:, jD], in0=kk_t[:, jD], in1=ag_t[:, jD], op=Op.mult)
            v.op("tensor_tensor", out=sqs[:], in0=ag_t[:, jD], in1=c["kab"][:], op=Op.mult)
            v.op("tensor_tensor", out=sqs[:], in0=sqs[:], in1=c["okab"][:], op=Op.add)
            v.op("tensor_tensor", out=k_t[:, jD], in0=k_t[:, jD], in1=sqs[:], op=Op.mult)
        if "dbg_rt" in io and w == 0:
            dsem = cx.sem("dbg")
            for nm, buf in (("dbg_rt", r_t), ("dbg_kt", k_t), ("dbg_vt", v_t),
                            ("dbg_ag", ag_t), ("dbg_gt", g_t), ("dbg_kkn", kk_t), ("dbg_bt", b_t)):
                y.dma(io[nm], buf[:, 0:NT * D], dsem)
        if stages < 3:
            continue
        s0 = mark('a3kk', s0)
        def t_fm(src, dst):
            for j in range(NT):
                for dc0 in range(0, NDC, 8):
                    for dd in range(8):
                        t.op("transpose", out=ps_bt[:, dd * 128:(dd + 1) * 128],
                             in_=src[:, j * D + (dc0 + dd) * 128:j * D + (dc0 + dd + 1) * 128],
                             identity=c["ident_b"][:])
                    v.op("tensor_copy",
                         out=apx(dst, 0, 128, dc0 * WA + j * 128, [(WA, 8), (1, 128)]),
                         in_=ps_bt.ap().rearrange("p (dc q) -> p dc q", q=128))
        t_fm(kk_t, At_f)
        t_fm(b_t, Bh_f)
        t_fm(k_t, Kh_f)
        t_fm(r_t, Rt_f)
        if stages < 4:
            continue
        s0 = mark('a4tfm', s0)
        for fc in range(NDC):
            fs = slice(fc * WA, (fc + 1) * WA)
            v.op("tensor_tensor_scan", out=cwc[:], data0=sig[:, fs], data1=sig[:, fs],
                 initial=0.0, op0=Op.add, op1=Op.bypass)
            v.op("tensor_copy", out=base[:, 1:NCH], in_=cwc[:, C - 1:WA - 1:C])
            v.op("memset", ap=base[:, 0:1], constant=0.0)
            v.op("tensor_tensor", out=cwc[:],
                 in0=cwc[:].rearrange("p (ch q) -> p ch q", ch=NCH),
                 in1=base.ap()[:, 0:NCH].unsqueeze(2).broadcast_to([128, NCH, C]), op=Op.subtract)
            v.op("tensor_copy", out=base[:, NCH:2 * NCH], in_=cwc[:, C - 1:WA:C])
            s.op("activation", out=gam[:, fc * NCH:(fc + 1) * NCH],
                 in_=base[:, NCH:2 * NCH], func=AF.Exp, scale=-DK)
            v.op("tensor_tensor", out=ebuf[:, 0:WA], in0=cwc[:], in1=sig[:, fs], op=Op.subtract)
            s.op("activation", out=ebuf[:, 0:WA], in_=ebuf[:, 0:WA], func=AF.Exp, scale=-DK)
            s.op("activation", out=ebuf[:, WA:2 * WA], in_=cwc[:], func=AF.Exp, scale=DK)
            s.op("activation", out=ebuf[:, 2 * WA:3 * WA], in_=cwc[:], func=AF.Exp, scale=-DK)
            v.op("scalar_tensor_tensor", out=At_f[:, fs], in0=At_f[:, fs], scalar=-1.0,
                 in1=ebuf[:, 0:WA], op0=Op.mult, op1=Op.mult)
            v.op("tensor_tensor", out=Rt_f[:, fs], in0=Rt_f[:, fs], in1=ebuf[:, 2 * WA:3 * WA], op=Op.mult)
            v.op("tensor_tensor", out=ebuf[:, 2 * WA:3 * WA],
                 in0=cwc[:].rearrange("p (ch q) -> p ch q", ch=NCH),
                 in1=base.ap()[:, NCH:2 * NCH].unsqueeze(2).broadcast_to([128, NCH, C]),
                 op=Op.subtract)
            s.op("activation", out=ebuf[:, 2 * WA:3 * WA], in_=ebuf[:, 2 * WA:3 * WA],
                 func=AF.Exp, scale=DK)
            v.op("tensor_tensor", out=BKb_f[:, 0:WA], in0=Bh_f[:, fs], in1=ebuf[:, 2 * WA:3 * WA], op=Op.mult)
            v.op("tensor_tensor", out=BKb_f[:, WA:2 * WA], in0=Kh_f[:, fs], in1=ebuf[:, 2 * WA:3 * WA], op=Op.mult)
            v.op("tensor_tensor", out=Bh_f[:, fs], in0=Bh_f[:, fs], in1=ebuf[:, WA:2 * WA], op=Op.mult)
            v.op("tensor_tensor", out=Kh_f[:, fs], in0=Kh_f[:, fs], in1=ebuf[:, WA:2 * WA], op=Op.mult)
            for (srcap, dst) in ((At_f[:, fs], ag_t), (BKb_f[:, 0:WA], kk_t), (BKb_f[:, WA:2 * WA], b_t)):
                for j in range(NT):
                    t.op("transpose", out=ps_bt[:, (j % 8) * 128:(j % 8 + 1) * 128],
                         in_=srcap[:, j * 128:(j + 1) * 128], identity=c["ident_b"][:])
                for j in range(NT):
                    s.op("copy", out=dst[:, j * D + fc * 128:j * D + (fc + 1) * 128],
                         in_=ps_bt[:, (j % 8) * 128:(j % 8 + 1) * 128])
        At_t, Bb_t, Kb_t = ag_t, kk_t, b_t
        if "dbg_sig" in io and w == 0:
            dsem2 = cx.sem("dbg2")
            y.dma(io["dbg_sig"], sig[:, 0:NDC * WA], dsem2)
            y.dma(io["dbg_At"], At_f[:, 0:NDC * WA], dsem2)
            y.dma(io["dbg_Bh"], Bh_f[:, 0:NDC * WA], dsem2)
            y.dma(io["dbg_Att"], At_t[:, 0:NT * D], dsem2)
        if stages < 5:
            continue
        s0 = mark('a56scan', s0)
        for cp in range(NCP):
            c0, c1 = 2 * cp, 2 * cp + 1
            for par in (0, 1):
                hs = par * 64
                for hg in range(4):
                    h1 = par + 4 * hg
                    h2 = h1 + 2
                    pA = (ps_a, ps_c)[hg % 2]
                    pB = (ps_b, ps_d)[hg % 2]
                    for (hh, colb) in ((h1, 0), (h2, 256)):
                        ft = hh // 2
                        for (cc, pp, ocs) in ((c0, pA, 0), (c1, pB, 64)):
                            for (ii, lb, rb) in ((0, Bh_f, At_f), (1, Bh_f, Rt_f),
                                                 (2, Kh_f, At_f), (3, Kh_f, Rt_f)):
                                t.op("matmul", out=pp[ocs:ocs + 64, colb + ii * 64:colb + (ii + 1) * 64],
                                     lhsT=lb[hs:hs + 64, ft * WA + cc * C:ft * WA + (cc + 1) * C],
                                     rhs=rb[hs:hs + 64, ft * WA + cc * C:ft * WA + (cc + 1) * C],
                                     start=True, stop=True, tile_position=(hs, ocs))
                    mL = c["maskL"].ap().unsqueeze(1).unsqueeze(1).broadcast_to([64, 2, 2, 64])
                    mS = c["maskS"].ap().unsqueeze(1).unsqueeze(1).broadcast_to([64, 2, 2, 64])
                    for (pp, ro, co) in ((pA, 0, 0), (pB, 64, 64)):
                        v.op("tensor_tensor",
                             out=apx(LbLk, ro, 64, h1 * 256 + co, [(512, 2), (128, 2), (1, 64)]),
                             in0=apx(pp, ro, 64, 0, [(256, 2), (128, 2), (1, 64)]),
                             in1=mL, op=Op.mult)
                        v.op("tensor_tensor",
                             out=apx(SqSk, ro, 64, h1 * 128, [(256, 2), (64, 2), (1, 64)]),
                             in0=apx(pp, ro, 64, 64, [(256, 2), (128, 2), (1, 64)]),
                             in1=mS, op=Op.mult)
            if stages < 6:
                continue
            v.op("memset", ap=apx(LbLk, 0, 64, 64, [(128, 2 * H), (1, 64)]), constant=0.0)
            v.op("memset", ap=apx(LbLk, 64, 64, 0, [(128, 2 * H), (1, 64)]), constant=0.0)
            s0 = mark('a7cc', s0)
            pbank = [ps_c, ps_d, ps_e, ps_f]
            for q in range(4):
                h0 = 4 * q
                for u in range(4):
                    t.op("matmul", out=pbank[q][:, u * 64:(u + 1) * 64],
                         lhsT=LbLk[:, (h0 + u) * 256 + 128:(h0 + u + 1) * 256],
                         rhs=v_t[:, cp * D + (h0 + u) * 64:cp * D + (h0 + u + 1) * 64],
                         start=True, stop=True)
            for q in range(4):
                h0 = 4 * q
                v.op("tensor_copy", out=apx(Xb, 0, 128, h0 * 128, [(128, 4), (1, 64)]),
                     in_=At_t[:, cp * D + h0 * 64:cp * D + (h0 + 4) * 64])
                v.op("tensor_copy", out=apx(Xb, 0, 128, h0 * 128 + 64, [(128, 4), (1, 64)]),
                     in_=pbank[q][:, 0:256])
            for q in range(4):
                h0 = 4 * q
                for u in range(4):
                    t.op("transpose", out=ps_bt[:, ((q % 2) * 4 + u) * 128:((q % 2) * 4 + u + 1) * 128],
                         in_=LbLk[:, (h0 + u) * 256:(h0 + u) * 256 + 128],
                         identity=c["ident_b"][:])
                if q % 2 == 1:
                    v.op("tensor_copy", out=Rb[:, (q - 1) * 512:(q + 1) * 512],
                         in_=ps_bt[:, 0:1024])
            for lev in range(6):
                if lev == 0:
                    qsl = lambda hh: LbLk[:, hh * 256:hh * 256 + 128]
                else:
                    qsl = lambda hh: Qb[:, hh * 128:(hh + 1) * 128]
                for q in range(4):
                    h0 = 4 * q
                    for u in range(4):
                        t.op("matmul", out=pbank[q][:, u * 128:(u + 1) * 128],
                             lhsT=qsl(h0 + u),
                             rhs=Xb[:, (h0 + u) * 128:(h0 + u + 1) * 128],
                             start=True, stop=True)
                for q in range(4):
                    blk4 = slice(4 * q * 128, 4 * (q + 1) * 128)
                    v.op("tensor_tensor", out=Xb[:, blk4], in0=pbank[q][:, 0:512],
                         in1=Xb[:, blk4], op=Op.add)
                if lev < 5:
                    for q in range(4):
                        h0 = 4 * q
                        for u in range(4):
                            t.op("matmul", out=pbank[q][:, u * 128:(u + 1) * 128],
                                 lhsT=Rb[:, (h0 + u) * 128:(h0 + u + 1) * 128],
                                 rhs=qsl(h0 + u),
                                 start=True, stop=True)
                    if lev < 4:
                        for qq in range(4):
                            h0 = 4 * qq
                            for u in range(4):
                                t.op("matmul", out=ps_g[:, u * 128:(u + 1) * 128],
                                     lhsT=qsl(h0 + u),
                                     rhs=Rb[:, (h0 + u) * 128:(h0 + u + 1) * 128],
                                     start=True, stop=True)
                            s.op("copy", out=Rb[:, h0 * 128:(h0 + 4) * 128],
                                 in_=ps_g[:, 0:512])
                    for q in range(4):
                        s.op("copy", out=Qb[:, 4 * q * 128:4 * (q + 1) * 128],
                             in_=pbank[q][:, 0:512])
            Pt2 = {c0: ps_c, c1: ps_f}
            Gp2 = {c0: ps_d, c1: ps_g}
            Rp2 = {c0: ps_e, c1: ps_b}
            PtS = {c0: Pt[:, 0:512] if False else Pt, c1: Pt}
            for cc in (c0, c1):
                cs = (cc % 2) * 64
                tok = lambda buf, hh: buf[cs:cs + 64, cp * D + hh * 64:cp * D + (hh + 1) * 64]
                Xsl = lambda hh, a2, b2: Xb[cs:cs + 64, hh * 128 + a2:hh * 128 + b2]
                for hh in range(H):
                    hs2 = (hh % 2) * 64
                    hcol = slice((hh // 2) * 64, (hh // 2 + 1) * 64)
                    t.op("matmul", out=Pt2[cc][hs2:hs2 + 64, hcol], lhsT=Xsl(hh, 0, 64),
                         rhs=tok(Bb_t, hh), start=True, stop=True, tile_position=(cs, hs2))
                    t.op("matmul", out=Gp2[cc][hs2:hs2 + 64, hcol], lhsT=tok(Bb_t, hh),
                         rhs=Xsl(hh, 64, 128), start=True, stop=False, tile_position=(cs, hs2))
                    t.op("matmul", out=Gp2[cc][hs2:hs2 + 64, hcol], lhsT=tok(Kb_t, hh),
                         rhs=tok(v_t, hh), start=False, stop=True, tile_position=(cs, hs2))
                    t.op("matmul", out=Rp2[cc][hs2:hs2 + 64, hcol], lhsT=Xsl(hh, 0, 64),
                         rhs=SqSk[cs:cs + 64, hh * 128:hh * 128 + 64],
                         start=True, stop=True, tile_position=(cs, hs2))
            PtB = (Pt, Ptb2)
            RsB = (Rst, Rstb2)
            for ci, cc in enumerate((c0, c1)):
                s.op("copy", out=PtB[ci][:], in_=Pt2[cc][:])
                v.op("tensor_tensor", out=RsB[ci][:], in0=Rp2[cc][:],
                     in1=apx(Rt_f, 0, 128, cc * C, [(WA, 8), (1, 64)]), op=Op.add)
            for ci, cc in enumerate((c0, c1)):
                cs = (cc % 2) * 64
                tok = lambda buf, hh: buf[cs:cs + 64, cp * D + hh * 64:cp * D + (hh + 1) * 64]
                Xsl = lambda hh, a2, b2: Xb[cs:cs + 64, hh * 128 + a2:hh * 128 + b2]
                pRZlo, pRZhi = ps_a, ps_f
                pOlo, pOhi = ps_c, ps_e
                for hh in range(H):
                    hs2 = (hh % 2) * 64
                    pRZ = pRZlo if hh < 8 else pRZhi
                    t.op("matmul", out=pRZ[cs:cs + 64, (hh % 8) * 64:(hh % 8 + 1) * 64],
                         lhsT=RsB[ci][hs2:hs2 + 64, (hh // 2) * 64:(hh // 2 + 1) * 64],
                         rhs=Zs[hs2:hs2 + 64, (hh // 2) * 64:(hh // 2 + 1) * 64],
                         start=True, stop=True, tile_position=(hs2, cs))
                for hh in range(H):
                    pO = pOlo if hh < 8 else pOhi
                    t.op("matmul", out=pO[cs:cs + 64, (hh % 8) * 64:(hh % 8 + 1) * 64],
                         lhsT=SqSk[cs:cs + 64, hh * 128:hh * 128 + 64], rhs=Xsl(hh, 64, 128),
                         start=True, stop=False, tile_position=(cs, cs))
                    t.op("matmul", out=pO[cs:cs + 64, (hh % 8) * 64:(hh % 8 + 1) * 64],
                         lhsT=SqSk[cs:cs + 64, hh * 128 + 64:(hh + 1) * 128], rhs=tok(v_t, hh),
                         start=False, stop=True, tile_position=(cs, cs))
                s.op("copy", out=sqs[0:64, 0:512], in_=pRZlo[cs:cs + 64, 0:512])
                s.op("copy", out=sqs[0:64, 512:1024], in_=pRZhi[cs:cs + 64, 0:512])
                otile = (cc // 2)
                orow = O_t[:, otile * D:(otile + 1) * D]
                v.op("tensor_tensor", out=orow[cs:cs + 64, 0:512], in0=pOlo[cs:cs + 64, 0:512],
                     in1=sqs[0:64, 0:512], op=Op.add)
                v.op("tensor_tensor", out=orow[cs:cs + 64, 512:1024], in0=pOhi[cs:cs + 64, 0:512],
                     in1=sqs[0:64, 512:1024], op=Op.add)
                for hh in range(H):
                    hs2 = (hh % 2) * 64
                    hcol = slice((hh // 2) * 64, (hh // 2 + 1) * 64)
                    chp0 = ps_b if cc == c0 else ps_f
                    t.op("matmul", out=chp0[hs2:hs2 + 64, hcol],
                         lhsT=PtB[ci][hs2:hs2 + 64, hcol],
                         rhs=Zs[hs2:hs2 + 64, hcol], start=True, stop=True,
                         tile_position=(hs2, hs2))
                chp = ps_b if cc == c0 else ps_f
                v.op("tensor_tensor", out=Zt[:], in0=Zs[:],
                     in1=apx(gam, 0, 128, cc, [(NCH, 8), (0, 64)]), op=Op.mult)
                v.op("tensor_tensor", out=Zt[:], in0=chp[:], in1=Zt[:], op=Op.add)
                v.op("tensor_tensor", out=Zs[:], in0=Gp2[cc][:], in1=Zt[:], op=Op.add)
        for j in range(NT):
            jD = slice(j * D, (j + 1) * D)
            ot = O_t[:, jD]
            GS = lambda a: sums[:, 32 + a * 16:32 + (a + 1) * 16]
            v.op("tensor_reduce", out=GS(0), in_=ot.rearrange("p (h q) -> p h q", h=H),
                 axis=AX, op=Op.add)
            v.op("tensor_tensor", out=sqs[:], in0=ot, in1=ot, op=Op.mult)
            v.op("tensor_reduce", out=GS(1), in_=sqs.ap().rearrange("p (h q) -> p h q", h=H),
                 axis=AX, op=Op.add)
            v.op("tensor_scalar", out=GS(0), in0=GS(0), scalar1=1.0 / HD, scalar2=None, op0=Op.mult)
            v.op("tensor_scalar", out=GS(1), in0=GS(1), scalar1=1.0 / HD, scalar2=None, op0=Op.mult)
            v.op("tensor_tensor", out=sums[:, 0:16], in0=GS(0), in1=GS(0), op=Op.mult)
            v.op("tensor_tensor", out=GS(1), in0=GS(1), in1=sums[:, 0:16], op=Op.subtract)
            s.op("activation", out=GS(1), in_=GS(1), func=AF.Sqrt, bias=c["gnepsb"][:])
            v.op("reciprocal", out=GS(1), in_=GS(1))
            v.op("tensor_tensor", out=ot, in0=ot,
                 in1=GS(0).unsqueeze(2).broadcast_to([128, H, HD]), op=Op.subtract)
            v.op("tensor_tensor", out=ot, in0=ot,
                 in1=GS(1).unsqueeze(2).broadcast_to([128, H, HD]), op=Op.mult)
            v.op("tensor_tensor", out=ot, in0=ot, in1=c["gnwb"][:], op=Op.mult)
            v.op("tensor_tensor", out=ot, in0=ot, in1=c["gnbb"][:], op=Op.add)
            v.op("tensor_tensor", out=sqs[:], in0=r_t[:, jD], in1=k_t[:, jD], op=Op.mult)
            v.op("tensor_tensor", out=sqs[:], in0=sqs[:], in1=c["rkb"][:], op=Op.mult)
            v.op("tensor_reduce", out=sums[:, 0:16], in_=sqs.ap().rearrange("p (h q) -> p h q", h=H),
                 axis=AX, op=Op.add)
            v.op("tensor_tensor", out=sqs[:], in0=v_t[:, jD],
                 in1=sums.ap()[:, 0:16].unsqueeze(2).broadcast_to([128, H, HD]), op=Op.mult)
            v.op("tensor_tensor", out=ot, in0=ot, in1=sqs[:], op=Op.add)
            v.op("tensor_tensor", out=un1[:, jD], in0=ot, in1=g_t[:, jD], op=Op.mult)
            for dc0 in range(0, NDC, 8):
                for dd in range(8):
                    t.op("transpose", out=ps_bt[:, dd * 128:(dd + 1) * 128],
                         in_=un1[:, j * D + (dc0 + dd) * 128:j * D + (dc0 + dd + 1) * 128],
                         identity=c["ident_b"][:])
                v.op("tensor_copy",
                     out=Rb.ap().rearrange("p (dc q) -> p dc q", q=128)[:, dc0:dc0 + 8, 0:128]
                     if False else apx(Rb, 0, 128, dc0 * 128, [(128, 8), (1, 128)]),
                     in_=ps_bt[:, 0:1024])
            y.dma(xtile[:], x_in[(tw0 + j) * 128:(tw0 + j + 1) * 128, :], ld)
            if j == 0:
                for nh in range(4):
                    y.dma((wprojB + wprojdB)[nh].ap().rearrange("p (dc q) -> p dc q", q=256),
                          WoB[:, nh], wl)
            for nh in range(4):
                wproj = (wprojB + wprojdB)[nh % 4]
                pp = (ps_c, ps_d)[nh % 2]
                for dc in range(NDC):
                    t.op("matmul", out=pp[:, 0:256], lhsT=Rb[:, dc * 128:(dc + 1) * 128],
                         rhs=wproj[:, dc * 256:(dc + 1) * 256],
                         start=(dc == 0), stop=(dc == NDC - 1))
                v.op("tensor_tensor", out=xtile[:, nh * 256:(nh + 1) * 256],
                     in0=pp[:, 0:256], in1=xtile[:, nh * 256:(nh + 1) * 256], op=Op.add)
            y.dma(x_out[(tw0 + j) * 128:(tw0 + j + 1) * 128, :], xtile[:], st)
        s0 = mark('a10gn', s0)



def build_module(nc, io, T, dev=False, wait_targets=None):
    cx = Ctx(nc, dev=dev)
    cx.wait_targets = wait_targets
    cx.engines()
    blk = cx.es.enter_context(nc.Block())
    ph_consts(cx, io)
    with cx.phase("wq"):
        ph_wq(cx, io, mid=lambda: ph_wcast(cx, io))
    with cx.phase("ffn1"):
        ph_ffn(cx, io, io["x"], io["x1"], T, "f1")
    with cx.phase("attn"):
        ph_attn(cx, io, io["x1"], io["x2"], T, "a1")
    with cx.phase("ffn2"):
        ph_ffn(cx, io, io["x2"], io["out"], T, "f2")
    cx.fence()
    cx.es.close()
    return cx


def declare_io(nc, T, params_shapes=None):
    io = {}
    io["x"] = nc.dram_tensor("x", [T, D], F32, kind="ExternalInput").ap()
    io["wg"] = nc.dram_tensor("wg", [I_FF, D], F32, kind="ExternalInput").ap()
    io["wu"] = nc.dram_tensor("wu", [I_FF, D], F32, kind="ExternalInput").ap()
    io["wd"] = nc.dram_tensor("wd", [D, I_FF], F32, kind="ExternalInput").ap()
    for nm in ("Wr", "Wk", "Wv", "Wo"):
        io[nm] = nc.dram_tensor(nm, [D, D], F32, kind="ExternalInput").ap()
        io[nm + "b"] = nc.dram_tensor(nm + "b", [128, 4, NDC, 256], BF16, kind="Internal").ap()
        if nm != "Wo":
            io[nm + "d"] = nc.dram_tensor(nm + "d", [128, 4, NDC, 256], BF16, kind="Internal").ap()
    io["wguT"] = nc.dram_tensor("wguT", [128, I_FF // 256, 16, 256], BF16, kind="Internal").ap()
    io["wdT5"] = nc.dram_tensor("wdT5", [128, 4, NIC, 256], BF16, kind="Internal").ap()
    io["x1"] = nc.dram_tensor("x1", [T, D], F32, kind="Internal").ap()
    io["x2"] = nc.dram_tensor("x2", [T, D], F32, kind="Internal").ap()
    io["out"] = nc.dram_tensor("out", [T, D], F32, kind="ExternalOutput").ap()
    return io


CONST_NAMES_BF = ["ident_b", "ones_row", "kkb", "kab", "okab", "rkb", "gnwb", "gnbb",
                  "wB_aug", "aB_aug", "gB_c", "wA_c", "aA_c", "gA_c", "maskL", "maskS"]


def declare_const_io(nc, io, hc):
    import ml_dtypes
    for nm, arr in hc.items():
        dt = BF16 if arr.dtype == ml_dtypes.bfloat16 else F32
        io[nm] = nc.dram_tensor(f"c_{nm}", list(arr.shape), dt, kind="ExternalInput").ap()



from concourse.bass_utils import run_bass_kernel_spmd

T_FULL = 4096


def kernel(**inputs):
    x_all = np.ascontiguousarray(np.asarray(inputs["L0_attn_hidden_states"], dtype=np.float32))
    B = x_all.shape[0]
    assert B == 8 and x_all.shape[1] == T_FULL and x_all.shape[2] == D

    pmap = {
        "mix_r": "L0_attn_mix_r", "mix_w": "L0_attn_mix_w", "mix_k": "L0_attn_mix_k",
        "mix_v": "L0_attn_mix_v", "mix_a": "L0_attn_mix_a", "mix_g": "L0_attn_mix_g",
        "k_k": "L0_attn_k_k", "k_a": "L0_attn_k_a", "r_k": "L0_attn_r_k",
        "gn_weight": "L0_attn_gn_weight", "gn_bias": "L0_attn_gn_bias",
        "wA": "L0_attn_wA", "wB": "L0_attn_wB", "w_bias": "L0_attn_w_bias",
        "aA": "L0_attn_aA", "aB": "L0_attn_aB", "a_bias": "L0_attn_a_bias",
        "gA": "L0_attn_gA", "gB": "L0_attn_gB",
    }
    params = {k: np.asarray(inputs[v], dtype=np.float32) for k, v in pmap.items()}
    hc = host_consts(params)

    nc0 = bass.Bass("TRN2", target_bir_lowering=False, debug=False,
                    detect_race_conditions=False)
    io0 = declare_io(nc0, T_FULL)
    declare_const_io(nc0, io0, hc)
    cx0 = build_module(nc0, io0, T_FULL, dev=False)
    wt = {k: set(v) for k, v in cx0.rec_targets.items()}

    nc = bass.Bass("TRN2", target_bir_lowering=False, debug=False,
                   detect_race_conditions=False)
    io = declare_io(nc, T_FULL)
    declare_const_io(nc, io, hc)
    build_module(nc, io, T_FULL, dev=False, wait_targets=wt)

    shared = dict(
        wg=np.asarray(inputs["L0_ffn_w_gate"], dtype=np.float32),
        wu=np.asarray(inputs["L0_ffn_w_up"], dtype=np.float32),
        wd=np.asarray(inputs["L0_ffn_w_down"], dtype=np.float32),
        Wr=np.asarray(inputs["L0_attn_Wr"], dtype=np.float32),
        Wk=np.asarray(inputs["L0_attn_Wk"], dtype=np.float32),
        Wv=np.asarray(inputs["L0_attn_Wv"], dtype=np.float32),
        Wo=np.asarray(inputs["L0_attn_Wo"], dtype=np.float32),
        **{f"c_{k}": v for k, v in hc.items()},
    )
    in_maps = [dict(x=x_all[b], **shared) for b in range(B)]
    res = run_bass_kernel_spmd(nc, in_maps, core_ids=list(range(B)))
    out = np.stack([np.asarray(res.results[b]["out"]) for b in range(B)], axis=0)
    return out.astype(np.float32)
